# version 08
# baseline (speedup 1.0000x reference)
import math
import numpy as np
import ml_dtypes
from contextlib import ExitStack

import concourse.bass as bass
import concourse.mybir as mybir
from concourse.bass_utils import run_bass_kernel_spmd

F32 = mybir.dt.float32
BF16 = mybir.dt.bfloat16
I32 = mybir.dt.int32
AF = mybir.ActivationFunctionType
ALU = mybir.AluOpType
AX = mybir.AxisListType

P = 128
D = 2048
T = 1024
NCH = D // P
NHEAD = 16
IC = 43
LAMBDA_INIT = 0.8 - 0.6 * math.exp(-0.3 * 12)
SCALE_A = 1.0 / math.sqrt(128.0)
SCALE_B = math.sqrt(128.0)
SHIFT_B = -20.0
EPS = 1e-5
GT = 86
SHUF = [(i + 16) % 32 for i in range(32)]



def _perm64():
    p = np.zeros(64, np.int64)
    for i in range(64):
        if i < 16:
            f, s = i, 0
        elif i < 32:
            f, s = i - 16, 1
        elif i < 48:
            f, s = i - 16, 0
        else:
            f, s = i - 32, 1
        p[i] = 2 * f + s
    return p


def _pan(w):
    O = w.shape[0]
    return np.ascontiguousarray(w.reshape(O // P, P, D).transpose(0, 2, 1))


def _rope_tabs(tok):
    invf = 10000.0 ** (-(2 * np.arange(32, dtype=np.float64)) / 64.0)
    Ct = np.zeros((P, T), np.float64)
    St = np.zeros((P, T), np.float64)
    for p in range(P):
        pp = p % 64
        if pp < 16:
            f, xi = pp, 0
        elif pp < 32:
            f, xi = pp - 16, 1
        elif pp < 48:
            f, xi = pp - 16, 0
        else:
            f, xi = pp - 32, 1
        ang = tok.astype(np.float64) * invf[f]
        Ct[p] = np.cos(ang)
        St[p] = np.sin(ang) * (1.0 if xi else -1.0)
    return Ct.astype(np.float32), St.astype(np.float32)


def _tile_lists():
    A = [(8 * a + kk, a, kk) for a in range(4) for kk in range(4)]
    Bl = []
    for j in range(32):
        R, tt = j // 8, j % 8
        a = R if tt < 4 else 7 - R
        Bl.append((j, a, tt % 4))
    return A, Bl


def _thr_table(r):
    A, Bl = _tile_lists()
    k = np.arange(P)
    cols = []
    for qa, lst in ((r, A), (7 - r, Bl)):
        for (j, a, kk) in lst:
            t = 128 * kk + 512 * (a - qa) + k
            for h in (0, 1):
                tl = t - 256 * h
                tl = np.where(tl <= 0, 0, np.where(tl >= 256, 1000, tl))
                cols.append(tl)
    return np.ascontiguousarray(np.stack(cols, axis=1).astype(np.float32))


def _ghost_tiles(g, u, d, c):
    out = np.empty((3 * GT, P, P), np.int32)
    n = 0
    for t in range(GT * c, GT * (c + 1)):
        ip, ht = t // 16, t % 16
        out[n] = g[128 * ip:128 * ip + 128, 128 * ht:128 * ht + 128].T
        n += 1
    for t in range(GT * c, GT * (c + 1)):
        ip, ht = t // 16, t % 16
        out[n] = u[128 * ip:128 * ip + 128, 128 * ht:128 * ht + 128].T
        n += 1
    for t in range(GT * c, GT * (c + 1)):
        ob, it = t // 43, t % 43
        out[n] = d[128 * ob:128 * ob + 128, 128 * it:128 * it + 128].T
        n += 1
    return np.ascontiguousarray(out)


def host_prep(inp):
    x = np.asarray(inp["L0_attn_a_hidden_states"], np.float32)
    w_qkv = np.asarray(inp["L0_attn_a_w_qkv"], np.float32)
    w_o = np.asarray(inp["L0_attn_a_w_o"], np.float32)
    wq = np.asarray(inp["L0_attn_b_wq"], np.float32)
    wk = np.asarray(inp["L0_attn_b_wk"], np.float32)
    wv = np.asarray(inp["L0_attn_b_wv"], np.float32)
    wout = np.asarray(inp["L0_attn_b_wout"], np.float32)
    lut = np.asarray(inp["L0_ffn_lut"], np.float32).reshape(1, 16)
    gidx = np.asarray(inp["L0_ffn_gate_idx"], np.int32)
    uidx = np.asarray(inp["L0_ffn_up_idx"], np.int32)
    didx = np.asarray(inp["L0_ffn_down_idx"], np.int32)

    perm = _perm64()
    permD = np.concatenate([64 * hh + perm for hh in range(32)])

    shared = {
        "wqkA": _pan(w_qkv[0:4096]),
        "wvA": np.ascontiguousarray(w_qkv[4096:6144].T),
        "woA": _pan(w_o),
        "wqB": _pan(wq[permD]),
        "wkB": _pan(wk[permD]),
        "wvB": np.ascontiguousarray(wv.T),
        "woB": _pan(wout),
        "lut": lut,
        "lamv": np.concatenate([
            np.asarray(inp["L0_attn_b_lambda_q1"], np.float32),
            np.asarray(inp["L0_attn_b_lambda_k1"], np.float32),
            np.asarray(inp["L0_attn_b_lambda_q2"], np.float32),
            np.asarray(inp["L0_attn_b_lambda_k2"], np.float32),
        ]).reshape(1, 256),
        "sublnw": np.asarray(inp["L0_attn_b_subln_weight"], np.float32).reshape(128, 1),
        "iota": np.ascontiguousarray(
            np.tile(np.arange(256, dtype=np.float32), (P, 1)).astype(ml_dtypes.bfloat16)),
    }

    in_maps, toks = [], []
    for c in range(8):
        g, r = c // 4, c % 4
        tok = np.r_[512 * r:512 * r + 512, 512 * (7 - r):512 * (8 - r)]
        toks.append((g, tok))
        Ct, St = _rope_tabs(tok)
        m = dict(shared)
        m["xin"] = np.ascontiguousarray(x[g][tok].T)
        m["ctab"] = np.ascontiguousarray(np.concatenate([Ct, St], axis=1))
        m["thr"] = _thr_table(r)
        m["bidx"] = _ghost_tiles(gidx, uidx, didx, c)
        in_maps.append(m)
    return in_maps, toks



class Sem:
    def __init__(self, h):
        self.h = h
        self.n = 0


def build_nc(upto="out"):
    nc = bass.Bass("TRN2", target_bir_lowering=False)
    es = ExitStack()

    di = lambda name, shape, dt: nc.dram_tensor(name, shape, dt, kind="ExternalInput").ap()
    d_xin = di("xin", [D, T], F32)
    d_wqkA = di("wqkA", [32, D, P], F32)
    d_wvA = di("wvA", [D, D], F32)
    d_woA = di("woA", [16, D, P], F32)
    d_wqB = di("wqB", [16, D, P], F32)
    d_wkB = di("wkB", [16, D, P], F32)
    d_wvB = di("wvB", [D, D], F32)
    d_woB = di("woB", [16, D, P], F32)
    d_lut = di("lut", [1, 16], F32)
    d_lamv = di("lamv", [1, 256], F32)
    d_sublnw = di("sublnw", [P, 1], F32)
    d_iota = di("iota", [P, 256], BF16)
    d_ctab = di("ctab", [P, 2048], F32)
    d_thr = di("thr", [P, 96], F32)
    d_bidx = di("bidx", [3 * GT, P, P], I32)
    d_out = nc.dram_tensor("out", [D, T], F32, kind="ExternalOutput").ap()
    DBG = (upto == "dbg")
    if DBG:
        d_dlnb = nc.dram_tensor("dlnb", [P, NCH * T], BF16, kind="ExternalOutput").ap()
        d_dkh = nc.dram_tensor("dkh", [P, 4096], BF16, kind="ExternalOutput").ap()
        d_dvh = nc.dram_tensor("dvh", [P, 4096], BF16, kind="ExternalOutput").ap()
        d_dqh = nc.dram_tensor("dqh", [P, T], BF16, kind="ExternalOutput").ap()
        d_de = nc.dram_tensor("de", [P, 2048], BF16, kind="ExternalOutput").ap()
        d_dao = nc.dram_tensor("dao", [P, 8 * T], BF16, kind="ExternalOutput").ap()
    if upto == "dbgb":
        d_dkh = nc.dram_tensor("dkh", [P, 4096], BF16, kind="ExternalOutput").ap()
        d_dvh = nc.dram_tensor("dvh", [P, 4096], BF16, kind="ExternalOutput").ap()
        d_dqh = nc.dram_tensor("dqh", [P, T], BF16, kind="ExternalOutput").ap()
        d_dao = nc.dram_tensor("dao", [P, 8 * T], BF16, kind="ExternalOutput").ap()
        d_drcp = nc.dram_tensor("drcp", [P, T], F32, kind="ExternalOutput").ap()
        d_dsst = nc.dram_tensor("dsst", [P, 512], F32, kind="ExternalOutput").ap()
        d_dusq = nc.dram_tensor("dusq", [P, 512], BF16, kind="ExternalOutput").ap()
        d_duac = nc.dram_tensor("duac", [P, 512], F32, kind="ExternalOutput").ap()
    DBGF = (upto == "dbgf")
    DBGB = (upto == "dbgb")
    if DBGF:
        d_dln2 = nc.dram_tensor("dln2", [P, NCH * T], BF16, kind="ExternalOutput").ap()
        d_dh = nc.dram_tensor("dh", [P, IC * 512], BF16, kind="ExternalOutput").ap()
        d_dxs = nc.dram_tensor("dxs", [P, NCH * T], F32, kind="ExternalOutput").ap()
        d_dhsg = nc.dram_tensor("dhsg", [P, 1024], BF16, kind="ExternalOutput").ap()
        d_dgsl = nc.dram_tensor("dgsl", [P, 2048], BF16, kind="ExternalOutput").ap()
        d_dusl = nc.dram_tensor("dusl", [P, 2048], BF16, kind="ExternalOutput").ap()
        d_dpgu = nc.dram_tensor("dpgu", [P, 2048], F32, kind="ExternalOutput").ap()

    d_gsl = [nc.dram_tensor(f"gsl{i}", [GT * P, P], BF16).ap() for i in range(3)]
    d_gw = [nc.dram_tensor(f"gw{i}", [8 * GT * P, P], BF16).ap() for i in range(3)]
    d_kvi = [nc.dram_tensor(f"kvi{i}", [4096, T], BF16).ap() for i in range(2)]
    d_kvg = [nc.dram_tensor(f"kvg{i}", [4 * 4096, T], BF16).ap() for i in range(2)]

    streams = {k: [] for k in "tvsgy"}
    mksem = lambda name: Sem(es.enter_context(nc.semaphore(name)))
    sY, sW, sG, sT, sV, sS, sC = (mksem(k) for k in
                                  ("sY", "sW", "sG", "sT", "sV", "sS", "sC"))
    sYi = mksem("sYi")
    sBL = mksem("sBL")
    sBS = [mksem(f"sBS{i}") for i in range(3)]
    sCg = mksem("sCg")
    sKS = mksem("sKS")
    sKV = mksem("sKV")
    sSl = [mksem(f"sSl{i}") for i in range(2)]
    sVw = [mksem(f"sVw{i}") for i in range(2)]
    sFg = mksem("sFg")
    sFd = mksem("sFd")
    sOut = mksem("sOut")
    sX = mksem("sX")
    ENG_SEM = {"t": sT, "v": sV, "s": sS, "g": sG, "y": sY}

    def em(e, f):
        streams[e].append(f)

    def wait(e, sem, val=None):
        v = sem.n if val is None else val
        if v > 0:
            em(e, lambda E, s=sem.h, vv=v: E.wait_ge(s, vv))

    def _inc(i, sem, k):
        if sem is not None:
            i.then_inc(sem.h, k)
        return i

    def mm(out, lhsT, rhs, start, stop, sem=None, tp=None):
        if sem:
            sem.n += 1
        em('t', lambda E, o=out, l=lhsT, r=rhs, a=start, b=stop, s=sem, t=tp:
           _inc(E.matmul(o, l, r, start=a, stop=b, tile_position=t), s, 1))

    def act(out, in_, fn, bias=None, scale=None, sem=None):
        if sem:
            sem.n += 1
        kw = {}
        if bias is not None:
            kw["bias"] = bias
        if scale is not None:
            kw["scale"] = scale
        em('s', lambda E, o=out, i=in_, f=fn, kw=dict(kw), s=sem:
           _inc(E.activation(o, i, f, **kw), s, 1))

    def v_op(name, *args, sem=None):
        if sem:
            sem.n += 1
        em('v', lambda E, n=name, a=args, s=sem: _inc(getattr(E, n)(*a), s, 1))

    def dma(q, out, in_, sem):
        sem.n += 16
        em(q, lambda E, o=out, i=in_, s=sem: _inc(E.dma_start(out=o, in_=i), s, 16))

    def v_delay(cycles=256):
        em('v', lambda E, c=cycles: E.nop(cycle_cnt=c, nofuse=True))

    def t_pause(cycles=600):
        em('t', lambda E, c=cycles: E.nop(cycle_cnt=c, nofuse=True))

    def s_flush():
        em('s', lambda E: E.activation(flt[0:1, 0:1], cst[0:1, 0:1], AF.Copy))

    def v_flush():
        em('v', lambda E: E.memset(flt[0:1, 4:6], 0.0))

    def barrier():
        marks = {}
        for e in "tvsgy":
            s = ENG_SEM[e]
            s.n += 1
            em(e, lambda E, ss=s: _inc(E.nop(), ss, 1))
            marks[e] = s.n
        for e in "tvsgy":
            for o in "tvsgy":
                if o != e:
                    wait(e, ENG_SEM[o], marks[o])

    sb = lambda name, shape, dt: es.enter_context(nc.sbuf_tensor(name, shape, dt))
    xs = sb("xs", [P, NCH * T], F32)
    ctab = sb("ctab_s", [P, 2048], F32)
    thr = sb("thr_s", [P, 96], F32)
    iota = sb("iotab", [P, 256], BF16)
    onesb = sb("onesb", [P, P], BF16)
    ones1f = sb("ones1f", [1, P], F32)
    cst = sb("cst", [P, 2], F32)
    lutr = sb("lutr", [1, 16], F32)
    lutbc = sb("lutbc", [P, 16], F32)
    lamv = sb("lamv_s", [1, 256], F32)
    lam_bc = sb("lam_bc", [P, 1], F32)
    sublnw = sb("sublnw_s", [P, 1], F32)
    xb = [sb(f"xb{i}", [P, T], BF16) for i in range(2)]
    sq = [sb("sqb0", [P, T], BF16)] * 2
    mt = sb("mt", [P, T], F32)
    vt = sb("vt", [P, T], F32)
    rt = sb("rt", [P, T], F32)
    s4 = sb("s4", [P, T], F32)
    kvst = [sb("kvst0", [P, T], BF16)] * 2
    flt = sb("flt", [1, 8], F32)

    nc.const_aps.register(F32, SHIFT_B, cst[:, 0:1])
    nc.const_aps.register(F32, EPS, cst[:, 1:2])

    v_op("memset", onesb[:, :], 1.0)
    v_op("memset", ones1f[:, :], 1.0)
    v_op("memset", cst[:, 0:1], SHIFT_B)
    v_op("memset", cst[:, 1:2], EPS, sem=sV)
    ini_v = sV.n

    dma('y', xs[:, :].rearrange("p (c t) -> p c t", c=NCH),
        d_xin.rearrange("(c p) t -> p c t", p=P), sYi)
    dma('y', ctab[:, :], d_ctab, sYi)
    dma('y', thr[:, :], d_thr, sYi)
    dma('y', iota[:, :], d_iota, sYi)
    dma('y', lutr[:, :], d_lut, sYi)
    dma('y', lamv[:, :], d_lamv, sYi)
    dma('y', sublnw[:, :], d_sublnw, sYi)
    ini_y = sYi.n

    wait('s', sV, ini_v)
    wait('s', sYi, ini_y)

    pbc_ctx = ExitStack()
    p_bc = pbc_ctx.enter_context(nc.psum_tensor("p_bc", [P, 16], F32))
    wait('t', sV, ini_v)
    wait('t', sYi, ini_y)
    mm(p_bc[:, :], ones1f[:, :], lutr[:, :], True, True, sem=sT)
    wait('v', sT)
    v_delay()
    v_op("tensor_copy", lutbc[:, :], p_bc[:, :], sem=sV)
    lut_v = sV.n
    pbc_ctx.close()

    bctx = ExitStack()
    g_idx = [bctx.enter_context(nc.sbuf_tensor(f"gidx{i}", [P, 4 * P], I32, side="right")) for i in range(2)]
    g_idxb = bctx.enter_context(nc.sbuf_tensor("gidxb", [P, 4 * P], BF16, side="right"))
    g_acc = bctx.enter_context(nc.sbuf_tensor("gacc", [P, 4 * P], BF16, side="right"))
    g_tsj = bctx.enter_context(nc.sbuf_tensor("gtsj", [P, 4 * P], BF16, side="right"))

    groups = []
    for w in range(3):
        t0 = 0
        while t0 < GT:
            n = min(4, GT - t0)
            groups.append((w, t0, n))
            t0 += n

    ag_mark = [None, None, None]
    idx_free = [0, 0]

    def _gen():
        for gi, (w, t0, n) in enumerate(groups):
            buf, F = gi % 2, n * P
            yield ("load", (w, t0, n, buf, F))
            yield ("cast", (buf, F))
            for j in range(16):
                yield ("ts", (j, F))
                if j > 0:
                    yield ("tt", F)
            yield ("store", (w, t0, n, F))
            if (gi + 1) % 22 == 0:
                yield ("ag", w)

    bgen = _gen()
    bdone = [False]
    bclosed = [False]
    bload = [0]

    def pull_build(k):
        if bdone[0]:
            return
        for _ in range(k):
            try:
                kind, a = next(bgen)
            except StopIteration:
                bdone[0] = True
                return
            if kind == "load":
                w, t0, n, buf, F = a
                wait('y', sV, idx_free[buf])
                src = d_bidx[w * GT + t0:w * GT + t0 + n, :, :].rearrange("n p m -> p n m")
                dma('y', g_idx[buf][:, 0:F].rearrange("p (n m) -> p n m", n=n), src, sBL)
                bload[0] = sBL.n
            elif kind == "cast":
                buf, F = a
                wait('v', sBL, bload[0])
                wait('v', sV, lut_v)
                v_op("tensor_copy", g_idxb[:, 0:F], g_idx[buf][:, 0:F], sem=sV)
                idx_free[buf] = sV.n
            elif kind == "ts":
                j, F = a
                dst = g_acc if j == 0 else g_tsj
                v_op("tensor_scalar", dst[:, 0:F], g_idxb[:, 0:F], float(j),
                     lutbc[:, j:j + 1], ALU.is_equal, ALU.mult)
            elif kind == "tt":
                F = a
                v_op("tensor_add", g_acc[:, 0:F], g_acc[:, 0:F], g_tsj[:, 0:F])
            elif kind == "store":
                w, t0, n, F = a
                v_op("tensor_copy", g_tsj[:, 0:128], g_acc[:, 0:128], sem=sV)
                v_flush()
                wait('y', sV)
                dst = d_gsl[w][t0 * P:(t0 + n) * P, :].rearrange("(n p) m -> p n m", p=P)
                dma('y', dst, g_acc[:, 0:F].rearrange("p (n m) -> p n m", n=n), sBS[w])
            elif kind == "ag":
                w = a
                wait('g', sBS[w], sBS[w].n)
                sCg.n += 1
                em('g', lambda E, i=d_gsl[w], o=d_gw[w], s=sCg: _inc(
                    E.collective_compute("AllGather", ALU.bypass,
                                         replica_groups=[list(range(8))],
                                         ins=[i.opt()], outs=[o.opt()]), s, 1))
                ag_mark[w] = sCg.n

    ln_ctr = [0]

    def ln_phase(lnb, x_ready):
        xr_sem, xr_val = x_ready
        ln_ctr[0] += 1
        pctx = ExitStack()
        pdm = pctx.enter_context(nc.psum_tensor(f"pdm{ln_ctr[0]}", [P, T], F32))
        pdq = pctx.enter_context(nc.psum_tensor(f"pdq{ln_ctr[0]}", [P, T], F32))
        wait('s', xr_sem, xr_val)
        stat_mark = [0, 0]
        for hc in range(NCH):
            b = hc % 2
            if hc >= 2:
                wait('s', sT, stat_mark[b])
            act(xb[b][:, :], xs[:, T * hc:T * hc + T], AF.Copy, sem=sS)
            s_flush()
            wait('v', sS)
            v_delay()
            wait('v', sT, max(stat_mark))
            v_op("tensor_mul", sq[b][:, :], xb[b][:, :], xb[b][:, :], sem=sV)
            v_flush()
            wait('t', sV)
            t_pause()
            for qt in range(2):
                mm(pdm[:, 512 * qt:512 * qt + 512], onesb[:, :],
                   xb[b][:, 512 * qt:512 * qt + 512], hc == 0, hc == NCH - 1)
            for qt in range(2):
                mm(pdq[:, 512 * qt:512 * qt + 512], onesb[:, :],
                   sq[b][:, 512 * qt:512 * qt + 512], hc == 0, hc == NCH - 1,
                   sem=(sT if qt == 1 else None))
            stat_mark[b] = sT.n
        wait('v', sT)
        v_op("tensor_scalar_mul", mt[:, :], pdm[:, :], 1.0 / D)
        v_op("tensor_scalar_mul", vt[:, :], pdq[:, :], 1.0 / D)
        v_op("tensor_mul", rt[:, :], mt[:, :], mt[:, :])
        v_op("tensor_sub", vt[:, :], vt[:, :], rt[:, :], sem=sV)
        wait('s', sV)
        act(s4[:, :], vt[:, :], AF.Ln, bias=EPS)
        act(rt[:, :], s4[:, :], AF.Exp, scale=-0.5, sem=sS)
        s_flush()
        wait('v', sS)
        v_delay()
        for hc in range(NCH):
            v_op("tensor_sub", vt[:, :], xs[:, T * hc:T * hc + T], mt[:, :])
            v_op("tensor_mul", lnb[:, T * hc:T * hc + T], vt[:, :], rt[:, :],
                 sem=(sV if hc == NCH - 1 else None))
        v_flush()
        pctx.close()
        return sV.n

    slab_mark = {}

    def proj_fm(pq, pans, pidx, slab, slab_sem, rhs, rhs_rdy, pulls=0):
        key = slab.name if hasattr(slab, "name") else id(slab)
        wait('g', sT, slab_mark.get(key, 0))
        src = pans[pidx, :, :].rearrange("(c p) m -> p c m", p=P)
        dma('g', slab[:, :].rearrange("p (c m) -> p c m", c=NCH), src, slab_sem)
        ldv = slab_sem.n
        pull_build(pulls)
        wait('t', slab_sem, ldv)
        wait('t', rhs_rdy[0], rhs_rdy[1])
        t_pause()
        for hc in range(NCH):
            for qt in range(2):
                mm(pq[:, 512 * qt:512 * qt + 512], slab[:, P * hc:P * hc + P],
                   rhs[:, T * hc + 512 * qt:T * hc + 512 * qt + 512],
                   hc == 0, hc == NCH - 1,
                   sem=(sT if (hc == NCH - 1 and qt == 1) else None))
        slab_mark[key] = sT.n

    def attention(which, lnb, ln_rdy):
        a = which == 0
        kv_i, kv_g = d_kvi[which], d_kvg[which]
        kv6i = kv_i.rearrange("(kv tt p two) c -> kv tt two p c", kv=2, tt=8, p=P, two=2)
        kv6g = kv_g.rearrange("(r kv tt p two) c -> r kv two p tt c", r=4, kv=2, tt=8, p=P, two=2)
        AList, BList = _tile_lists()

        at = ExitStack()
        asb = lambda name, shape, dt: at.enter_context(nc.sbuf_tensor(name + str(which), shape, dt))
        qh = [asb(f"qh{i}", [P, T], BF16) for i in range(2)]
        Kh = asb("Kh", [P, 4096], BF16)
        Vh = asb("Vh", [P, 4096], BF16)
        esl = [asb(f"esl{i}", [P, 2048], BF16) for i in range(2)]
        wsl = [asb(f"wsl{i}", [P, 2048], BF16) for i in range(2)]
        ao8 = asb("ao8", [P, 8 * T], BF16)
        rcp = asb("rcp", [P, T], F32)
        uacc = asb("uacc", [P, 512], F32)
        ub = asb("ub", [P, 512], F32)
        usq = asb("usq", [P, T], BF16)
        sst = asb("sst", [P, T], F32)
        vw = [asb(f"vw{i}", [P, 512], BF16) for i in range(2)]
        vst = [asb("vstg0", [P, 512], BF16)] * 2

        p1 = ExitStack()
        pk = p1.enter_context(nc.psum_tensor(f"pk{which}", [P, T], F32))
        if not a:
            plam = p1.enter_context(nc.psum_tensor("plam", [P, 1], F32))
            wait('v', sYi, ini_y)
            v_op("tensor_mul", mt[0:1, 0:64], lamv[0:1, 0:64], lamv[0:1, 64:128])
            v_op("tensor_mul", mt[0:1, 64:128], lamv[0:1, 128:192], lamv[0:1, 192:256])
            v_op("reduce_sum", vt[0:1, 0:1], mt[0:1, 0:64], AX.X)
            v_op("reduce_sum", vt[0:1, 1:2], mt[0:1, 64:128], AX.X, sem=sV)
            wait('s', sV)
            act(s4[0:1, 0:2], vt[0:1, 0:2], AF.Exp, sem=sS)
            s_flush()
            wait('v', sS)
            v_delay()
            v_op("tensor_sub", vt[0:1, 0:1], s4[0:1, 0:1], s4[0:1, 1:2])
            v_op("tensor_scalar_add", vt[0:1, 0:1], vt[0:1, 0:1], LAMBDA_INIT, sem=sV)
            wait('t', sV)
            t_pause()
            mm(plam[:, :], ones1f[:, :], vt[0:1, 0:1], True, True, sem=sT)
            wait('v', sT)
            v_delay()
            v_op("tensor_copy", lam_bc[:, :], plam[:, :], sem=sV)

        kst_mark = [0, 0]
        for hc in range(NCH):
            pans = d_wqkA if a else d_wkB
            proj_fm(pk, pans, (16 + hc) if a else hc, wsl[hc % 2],
                    sSl[hc % 2], lnb, (sV, ln_rdy), pulls=3)
            b2 = hc % 2
            wait('s', sT)
            wait('s', sKS, max(kst_mark))
            if a:
                act(kvst[b2][:, :], pk[:, :], AF.Copy, sem=sS)
                s_flush()
                kdone, keng = sS.n, sS
            else:
                wait('s', sV)
                act(mt[:, :], pk[:, :], AF.Copy, sem=sS)
                s_flush()
                wait('v', sS)
                v_delay()
                wait('v', sKS, max(kst_mark))
                v_op("stream_shuffle", vt[:, :], mt[:, :], SHUF)
                v_op("tensor_mul", s4[:, :], mt[:, :], ctab[:, 0:T])
                v_op("tensor_mul", vt[:, :], vt[:, :], ctab[:, T:2 * T])
                v_op("tensor_add", kvst[b2][:, :], s4[:, :], vt[:, :], sem=sV)
                v_flush()
                kdone, keng = sV.n, sV
            wait('t', keng, kdone)
            wait('y', keng, kdone)
            dma('y', kv_i[P * hc:P * hc + P, :], kvst[b2][:, :], sKS)
            kst_mark[b2] = sKS.n
        p1.close()

        p2 = ExitStack()
        pv = p2.enter_context(nc.psum_tensor(f"pv{which}", [P, 8 * 512], F32))
        vsrc = d_wvA if a else d_wvB
        vst_mark = [0, 0]
        vw_mark = [0, 0]
        evac_mark = 0
        for dvc in range(4):
            for hc in range(NCH):
                b2 = hc % 2
                wait('g', sT, vw_mark[b2])
                dma('g', vw[b2][:, :], vsrc[P * hc:P * hc + P, 512 * dvc:512 * dvc + 512], sVw[b2])
                pull_build(1)
                wait('t', sVw[b2], sVw[b2].n)
                if dvc == 0 and hc == 0:
                    wait('t', sV, ln_rdy)
                    t_pause()
                    wait('t', sS)
                if hc == 0:
                    wait('t', sS, evac_mark)
                for tt in range(8):
                    mm(pv[:, 512 * tt:512 * tt + 512],
                       lnb[:, T * hc + P * tt:T * hc + P * tt + P],
                       vw[b2][:, :], hc == 0, hc == NCH - 1,
                       sem=(sT if tt == 7 else None))
                vw_mark[b2] = sT.n
            wait('s', sT)
            for tt in range(8):
                b3 = tt % 2
                wait('s', sKS, max(vst_mark))
                act(vst[b3][:, :], pv[:, 512 * tt:512 * tt + 512], AF.Copy, sem=sS)
                s_flush()
                wait('y', sS)
                dst = kv6i[1, tt, dvc // 2, :, 512 * (dvc % 2):512 * (dvc % 2) + 512]
                dma('y', dst, vst[b3][:, :], sKS)
                vst_mark[b3] = sKS.n
            evac_mark = sS.n
        vevac_mark = sS.n
        p2.close()

        wait('g', sKS, sKS.n)
        sC.n += 1
        em('g', lambda E, i=kv_i, o=kv_g, s=sC: _inc(
            E.collective_compute("AllGather", ALU.bypass,
                                 replica_groups=[[0, 1, 2, 3], [4, 5, 6, 7]],
                                 ins=[i.opt()], outs=[o.opt()]), s, 1))
        kv_ready = sC.n

        p3 = ExitStack()
        pscore = p3.enter_context(nc.psum_tensor(f"ps{which}", [P, 2048], F32))
        pav = p3.enter_context(nc.psum_tensor(f"pav{which}", [P, 512], F32))
        pd = p3.enter_context(nc.psum_tensor(f"pd{which}", [P, T], F32))
        pav2 = p3.enter_context(nc.psum_tensor(f"pav2{which}", [P, 512], F32)) if not a else None

        head_end = [0]
        ao_free = [0]
        epi_v = [0]

        def out_proj(group):
            pans = d_woA if a else d_woB
            wait('t', sV)
            t_pause()
            for oc in range(NCH):
                b2 = oc % 2
                key = wsl[b2].name
                wait('g', sT, slab_mark.get(key, 0))
                src = pans[oc, :, :].rearrange("(c p) m -> p c m", p=P)
                dma('g', wsl[b2][:, :].rearrange("p (c m) -> p c m", c=NCH), src, sSl[b2])
                wait('t', sSl[b2], sSl[b2].n)
                for qt in range(2):
                    for h in range(8):
                        hc = group * 8 + h
                        mm(pscore[:, 512 * qt:512 * qt + 512],
                           wsl[b2][:, P * hc:P * hc + P],
                           ao8[:, T * h + 512 * qt:T * h + 512 * qt + 512],
                           h == 0, h == 7, sem=(sT if h == 7 else None))
                slab_mark[key] = sT.n
                wait('v', sT)
                v_delay()
                for qt in range(2):
                    v_op("tensor_add", xs[:, T * oc + 512 * qt:T * oc + 512 * qt + 512],
                         xs[:, T * oc + 512 * qt:T * oc + 512 * qt + 512],
                         pscore[:, 512 * qt:512 * qt + 512],
                         sem=(sV if qt == 1 else None))
                v_flush()
                wait('t', sV)
            ao_free[0] = sT.n

        for hc in range(NHEAD):
            hb = hc % 2
            wait('y', sC, kv_ready)
            wait('y', sT, head_end[0])
            for R in range(4):
                dma('y', Kh[:, 1024 * R:1024 * R + 1024],
                    kv_g[4096 * R + P * hc:4096 * R + P * hc + P, :], sKV)
            for R in range(4):
                src = kv6g[R, 1, hc // 8, :, :, P * (hc % 8):P * (hc % 8) + P]
                dma('y', Vh[:, 1024 * R:1024 * R + 1024]
                    .rearrange("p (tt m) -> p tt m", tt=8), src, sKV)
            kvld = sKV.n

            pq = pscore[:, 0:T]
            if hc == 0:
                wait('t', sS, vevac_mark)
            proj_fm(pq, d_wqkA if a else d_wqB, hc, wsl[hb], sSl[hb],
                    lnb, (sV, ln_rdy), pulls=2)
            wait('s', sT)
            if a:
                act(qh[hb][:, :], pq, AF.Copy, sem=sS)
                s_flush()
                qrdy, qeng = sS.n, sS
            else:
                wait('s', sV)
                act(mt[:, :], pq, AF.Copy, sem=sS)
                s_flush()
                wait('v', sS)
                v_delay()
                v_op("stream_shuffle", vt[:, :], mt[:, :], SHUF)
                v_op("tensor_mul", s4[:, :], mt[:, :], ctab[:, 0:T])
                v_op("tensor_mul", vt[:, :], vt[:, :], ctab[:, T:2 * T])
                v_op("tensor_add", qh[hb][:, :], s4[:, :], vt[:, :], sem=sV)
                v_flush()
                qrdy, qeng = sV.n, sV
            wait('t', qeng, qrdy)
            t_pause()

            phases = [(0, AList, 0), (1, BList, 16)] if a else \
                     [(0, list(range(32)), None), (1, list(range(32)), None)]
            for (qw, tiles, thrbase) in phases:
                q0 = 512 * qw
                per = 4 if a else 2
                nblk = len(tiles) // per
                started = [False]
                exp_mark = {}
                for blk in range(nblk):
                    bt = tiles[per * blk:per * blk + per]
                    eb = esl[blk % 2]
                    if blk == 0:
                        wait('t', sKV, kvld)
                        wait('t', sV, epi_v[0])
                        t_pause()
                    if blk >= 1:
                        wait('t', sS, exp_mark[blk - 1])
                    for i2, tile in enumerate(bt):
                        j = tile[0] if a else tile
                        if a:
                            mm(pscore[:, 512 * i2:512 * i2 + 512],
                               Kh[:, P * j:P * j + P], qh[hb][:, q0:q0 + 512],
                               True, True, sem=(sT if i2 == per - 1 else None))
                        else:
                            mm(pscore[:, 1024 * i2:1024 * i2 + 512],
                               Kh[0:64, P * j:P * j + P], qh[hb][0:64, q0:q0 + 512],
                               True, True, tp=(0, 0))
                            mm(pscore[:, 1024 * i2 + 512:1024 * i2 + 1024],
                               Kh[64:P, P * j:P * j + P], qh[hb][64:P, q0:q0 + 512],
                               True, True, tp=(64, 0),
                               sem=(sT if i2 == per - 1 else None))
                    wait('s', sT)
                    act(eb[:, :], pscore[:, :], AF.Exp,
                        bias=(None if a else SHIFT_B),
                        scale=(SCALE_A if a else SCALE_B), sem=sS)
                    s_flush()
                    exp_mark[blk] = sS.n
                    if a:
                        wait('v', sS)
                        v_delay()
                        for i2, (j, aa, kk) in enumerate(bt):
                            tg = thrbase + per * blk + i2
                            for h2 in range(2):
                                c0 = 512 * i2 + 256 * h2
                                v_op("scalar_tensor_tensor", eb[:, c0:c0 + 256],
                                     iota[:, :], thr[:, 2 * tg + h2:2 * tg + h2 + 1],
                                     eb[:, c0:c0 + 256], ALU.is_ge, ALU.mult,
                                     sem=(sV if (i2 == per - 1 and h2 == 1) else None))
                        v_flush()
                        wait('t', sV)
                        t_pause(480)
                        pull_build(4)
                    else:
                        wait('t', sS)
                        t_pause(480)
                        pull_build(2)
                    last = blk == nblk - 1
                    st0 = not started[0]
                    started[0] = True
                    for i2, tile in enumerate(bt):
                        j = tile[0] if a else tile
                        fin = last and i2 == per - 1
                        if a:
                            mm(pd[:, q0:q0 + 512], onesb[:, :],
                               eb[:, 512 * i2:512 * i2 + 512], st0 and i2 == 0, fin)
                            mm(pav[:, :], Vh[:, P * j:P * j + P],
                               eb[:, 512 * i2:512 * i2 + 512], st0 and i2 == 0, fin,
                               sem=(sT if fin else None))
                        else:
                            s00 = st0 and i2 == 0
                            mm(pd[:, 0:512], onesb[:, :],
                               eb[:, 1024 * i2:1024 * i2 + 512], s00, fin)
                            mm(pd[:, 512:1024], onesb[:, :],
                               eb[:, 1024 * i2 + 512:1024 * i2 + 1024], s00, fin)
                            mm(pav[:, :], Vh[:, P * j:P * j + P],
                               eb[:, 1024 * i2:1024 * i2 + 512], s00, fin)
                            mm(pav2[:, :], Vh[:, P * j:P * j + P],
                               eb[:, 1024 * i2 + 512:1024 * i2 + 1024], s00, fin,
                               sem=(sT if fin else None))
                wait('s', sT)
                if a:
                    act(s4[:, 0:512], pd[:, q0:q0 + 512], AF.Ln)
                    act(rcp[:, q0:q0 + 512], s4[:, 0:512], AF.Exp, scale=-1.0, sem=sS)
                    s_flush()
                    wait('v', sS)
                    v_delay()
                    if hc >= 8:
                        wait('v', sT, ao_free[0])
                    v_op("tensor_mul", ao8[:, T * (hc % 8) + q0:T * (hc % 8) + q0 + 512],
                         pav[:, :], rcp[:, q0:q0 + 512], sem=sV)
                    v_flush()
                    epi_v[0] = sV.n
                    wait('t', sV, epi_v[0])
                else:
                    act(s4[:, 0:T], pd[:, :], AF.Ln)
                    act(rcp[:, :], s4[:, 0:T], AF.Exp, scale=-1.0, sem=sS)
                    s_flush()
                    wait('v', sS)
                    v_delay()
                    v_op("tensor_mul", uacc[:, :], pav[:, :], rcp[:, 0:512])
                    v_op("tensor_mul", ub[:, :], pav2[:, :], rcp[:, 512:1024])
                    v_op("tensor_scalar_mul", ub[:, :], ub[:, :], lam_bc[:, 0:1])
                    v_op("tensor_sub", uacc[:, :], uacc[:, :], ub[:, :])
                    v_op("tensor_mul", usq[:, q0:q0 + 512], uacc[:, :], uacc[:, :], sem=sV)
                    v_flush()
                    wait('t', sV)
                    t_pause()
                    wait('t', sS, exp_mark[nblk - 1])
                    mm(pscore[:, 0:512], onesb[:, :], usq[:, q0:q0 + 512],
                       True, True, sem=sT)
                    wait('s', sT)
                    act(rt[:, 0:512], pscore[:, 0:512], AF.Ln, bias=EPS, scale=1.0 / P)
                    act(sst[:, q0:q0 + 512], rt[:, 0:512], AF.Exp, scale=-0.5, sem=sS)
                    sX.n += 1
                    em('s', lambda E, s=sX: _inc(
                        E.activation(flt[0:1, 2:3], cst[0:1, 0:1], AF.Copy), s, 1))
                    wait('v', sX, sX.n)
                    wait('v', sS)
                    v_delay()
                    if hc >= 8:
                        wait('v', sT, ao_free[0])
                    v_op("tensor_mul", ub[:, :], uacc[:, :], sst[:, q0:q0 + 512])
                    v_op("tensor_scalar_mul",
                         ao8[:, T * (hc % 8) + q0:T * (hc % 8) + q0 + 512],
                         ub[:, :], sublnw[:, 0:1], sem=sV)
                    v_flush()
                    epi_v[0] = sV.n
                    wait('t', sV, epi_v[0])
            head_end[0] = sT.n
            if DBGB and which == 1 and hc == 15:
                wait('y', sV)
                wait('y', sS)
                dma('y', d_dkh, Kh[:, :], sOut)
                dma('y', d_dvh, Vh[:, :], sOut)
                dma('y', d_dqh, qh[hb][:, :], sOut)
                dma('y', d_dao, ao8[:, :], sOut)
                dma('y', d_drcp, rcp[:, :], sOut)
                dma('y', d_dsst, sst[:, :], sOut)
                dma('y', d_dusq, usq[:, :], sOut)
                dma('y', d_duac, uacc[:, :], sOut)
                wait('y', sOut, sOut.n)
                break
            if DBG and hc == 1:
                wait('y', sV)
                wait('y', sS)
                dma('y', d_dlnb, lnb[:, :], sOut)
                dma('y', d_dkh, Kh[:, :], sOut)
                dma('y', d_dvh, Vh[:, :], sOut)
                dma('y', d_dqh, qh[hb][:, :], sOut)
                dma('y', d_de, esl[(len(tiles) // per - 1) % 2][:, :], sOut)
                dma('y', d_dao, ao8[:, :], sOut)
                wait('y', sOut, sOut.n)
                break
            if hc % 8 == 7:
                out_proj(hc // 8)

        p3.close()
        at.close()
        return sV.n

    def ffn(lnb, ln_rdy, fidx):
        ft = ExitStack()
        fsb = lambda name, shape, dt: ft.enter_context(nc.sbuf_tensor(name + str(fidx), shape, dt))
        hbuf = fsb("hbuf", [P, IC * 512 + 512], BF16)
        gsl = [fsb("gsl0", [P, 2048], BF16)] * 2
        usl = [fsb("usl0", [P, 2048], BF16)] * 2
        dsl = [fsb(f"dsl{i}", [P, 2816], BF16) for i in range(2)]
        hsg = [fsb(f"hsg{i}", [P, 512], BF16) for i in range(2)]

        fp = ExitStack()
        pgu = [fp.enter_context(nc.psum_tensor(f"pgu{fidx}{i}", [P, T], F32)) for i in range(2)]
        pdn = [fp.enter_context(nc.psum_tensor(f"pdn{fidx}{i}", [P, 512], F32)) for i in range(2)]

        pull_build(10 ** 9)
        if not bclosed[0]:
            bctx.close()
            bclosed[0] = True
        g_rows = lambda w, ip: d_gw[w][2048 * ip:2048 * (ip + 1), :] \
            .rearrange("(c p) m -> p c m", p=P)

        gu_mark = [0, 0]
        pgu_mark = [0, 0]
        dn_mark = [0, 0]
        down_end = 0
        for half in range(2):
            q0 = 512 * half
            ic_order = list(range(IC)) if half == 0 else list(range(IC - 1, -1, -1))
            for icpos, ic in enumerate(ic_order):
                bb = icpos % 2
                if icpos == 0:
                    wait('y', sCg, ag_mark[0])
                    wait('y', sCg, ag_mark[1])
                wait('y', sT, max(gu_mark))
                dma('y', gsl[bb][:, :].rearrange("p (c m) -> p c m", c=NCH),
                    g_rows(0, ic), sFg)
                dma('y', usl[bb][:, :].rearrange("p (c m) -> p c m", c=NCH),
                    g_rows(1, ic), sFg)
                ldv = sFg.n
                wait('t', sFg, ldv)
                if icpos == 0:
                    wait('t', sV, ln_rdy)
                    wait('t', sV)
                    wait('t', sS)
                    t_pause()
                wait('t', sV, pgu_mark[bb])
                for hc in range(NCH):
                    mm(pgu[bb][:, 0:512], gsl[bb][:, P * hc:P * hc + P],
                       lnb[:, T * hc + q0:T * hc + q0 + 512], hc == 0, hc == NCH - 1)
                for hc in range(NCH):
                    mm(pgu[bb][:, 512:1024], usl[bb][:, P * hc:P * hc + P],
                       lnb[:, T * hc + q0:T * hc + q0 + 512], hc == 0, hc == NCH - 1,
                       sem=(sT if hc == NCH - 1 else None))
                gu_mark[bb] = sT.n
                wait('s', sT)
                act(hsg[bb][:, :], pgu[bb][:, 0:512], AF.Silu, sem=sS)
                s_flush()
                wait('v', sS)
                v_delay()
                v_op("tensor_mul", hbuf[:, 512 * ic:512 * ic + 512],
                     pgu[bb][:, 512:1024], hsg[bb][:, :], sem=sV)
                v_flush()
                pgu_mark[bb] = sV.n
            lic = ic_order[-1]
            v_op("tensor_copy", vt[:, 0:512], hbuf[:, 512 * lic:512 * lic + 512])
            v_op("tensor_copy", hbuf[:, 512 * lic:512 * lic + 512], vt[:, 0:512])
            v_op("tensor_copy", rt[:, 0:512], hbuf[:, 512 * lic:512 * lic + 512], sem=sV)
            v_flush()
            h_rdy = sV.n
            import os as _os
            if _os.environ.get("K_NO_DOWN") and half == 1:
                wait('y', sV, h_rdy)
                dma('y', d_dln2, lnb[:, :], sOut)
                dma('y', d_dh, hbuf[:, 0:IC * 512], sOut)
                dma('y', d_dxs, xs[:, :], sOut)
                wait('y', sOut, sOut.n)
                break
            if DBGF and half == 1:
                wait('s', sV, h_rdy)
                act(mt[:, :], pgu[0][:, :], AF.Copy)
                act(vt[:, :], pgu[1][:, :], AF.Copy, sem=sS)
                wait('y', sS)
                wait('y', sV, h_rdy)
                dma('y', d_dln2, lnb[:, :], sOut)
                dma('y', d_dh, hbuf[:, 0:IC * 512], sOut)
                dma('y', d_dxs, xs[:, :], sOut)
                dma('y', d_dhsg[:, 0:512], hsg[0][:, :], sOut)
                dma('y', d_dhsg[:, 512:1024], hsg[1][:, :], sOut)
                dma('y', d_dgsl, gsl[0][:, :], sOut)
                dma('y', d_dusl, usl[0][:, :], sOut)
                dma('y', d_dpgu[:, 0:1024], mt[:, :], sOut)
                dma('y', d_dpgu[:, 1024:2048], vt[:, :], sOut)
            for oc in range(NCH):
                if oc == 0:
                    wait('y', sCg, ag_mark[2])
                    wait('y', sV, h_rdy)
                for pc in range(2):
                    db = (oc * 2 + pc) % 2
                    wait('y', sT, dn_mark[db])
                    n_ic = 22 if pc == 0 else 21
                    rows = d_gw[2][5504 * oc + 2816 * pc:
                                   5504 * oc + 2816 * pc + 128 * n_ic, :] \
                        .rearrange("(c p) m -> p c m", p=P)
                    dma('y', dsl[db][:, 0:128 * n_ic]
                        .rearrange("p (c m) -> p c m", c=n_ic), rows, sFd)
                ldv = sFd.n
                if oc == 0:
                    wait('t', sV, h_rdy)
                    t_pause()
                wait('t', sFd, ldv)
                pb = oc % 2
                for dpos, ic in enumerate(ic_order):
                    pc = 0 if ic < 22 else 1
                    db = (oc * 2 + pc) % 2
                    col = P * (ic - 22 * pc)
                    mm(pdn[pb][:, :], dsl[db][:, col:col + P],
                       hbuf[:, 512 * ic:512 * ic + 512], dpos == 0, dpos == IC - 1,
                       sem=(sT if dpos == IC - 1 else None))
                dn_mark[0] = dn_mark[1] = sT.n
                wait('v', sT)
                v_delay()
                v_op("tensor_add", xs[:, T * oc + q0:T * oc + q0 + 512],
                     xs[:, T * oc + q0:T * oc + q0 + 512], pdn[pb][:, :], sem=sV)
                v_flush()
                wait('t', sV)
            down_end = sT.n
        fp.close()
        ft.close()
        return sV.n

    x_ready = (sYi, ini_y)
    stages = ["a", "f1", "b", "f2"] if upto == "out" else \
        {"x1": ["a"], "x2": ["a", "f1"], "x3": ["a", "f1", "b"], "dbg": ["a"],
         "dbgf": ["a", "f1"], "dbgb": ["a", "f1", "b"]}[upto]

    for st in stages:
        ctx = ExitStack()
        lnb = ctx.enter_context(nc.sbuf_tensor("lnb_" + st, [P, NCH * T], BF16))
        ln_rdy = ln_phase(lnb, x_ready)
        if st == "a":
            mark = attention(0, lnb, ln_rdy)
        elif st == "b":
            mark = attention(1, lnb, ln_rdy)
        elif st == "f1":
            mark = ffn(lnb, ln_rdy, 0)
        else:
            mark = ffn(lnb, ln_rdy, 1)
        x_ready = (sV, mark)
        ctx.close()
        barrier()

    wait('y', x_ready[0], x_ready[1])
    dma('y', d_out.rearrange("(c p) t -> p c t", p=P),
        xs[:, :].rearrange("p (c t) -> p c t", c=NCH), sOut)
    wait('y', sOut, sOut.n)

    with nc.Block() as block:
        @block.tensor
        def _(E):
            for f in streams['t']:
                f(E)

        @block.vector
        def _(E):
            for f in streams['v']:
                f(E)

        @block.scalar
        def _(E):
            for f in streams['s']:
                f(E)

        @block.gpsimd
        def _(E):
            for f in streams['g']:
                f(E)

        @block.sync
        def _(E):
            for f in streams['y']:
                f(E)

    n_inst = sum(len(v) for v in streams.values())
    print(f"[build] upto={upto} instructions ~{n_inst}")
    return nc


def kernel(**inputs):
    in_maps, toks = host_prep(inputs)
    nc = build_nc("out")
    res = run_bass_kernel_spmd(nc, in_maps, core_ids=list(range(8)))
    out = np.zeros((2, 4096, 2048), np.float32)
    for c, r in enumerate(res.results):
        g, tok = toks[c]
        out[g][tok] = np.asarray(r["out"]).T
    return out


# version 14
# speedup vs baseline: 1.1288x; 1.1288x over previous
import math
import numpy as np
import ml_dtypes
from contextlib import ExitStack

import concourse.bass as bass
import concourse.mybir as mybir
from concourse.bass_utils import run_bass_kernel_spmd

F32 = mybir.dt.float32
BF16 = mybir.dt.bfloat16
I32 = mybir.dt.int32
AF = mybir.ActivationFunctionType
ALU = mybir.AluOpType
AX = mybir.AxisListType

P = 128
D = 2048
T = 1024
NCH = D // P
NHEAD = 16
IC = 43
LAMBDA_INIT = 0.8 - 0.6 * math.exp(-0.3 * 12)
SCALE_A = 1.0 / math.sqrt(128.0)
SCALE_B = math.sqrt(128.0)
SHIFT_B = -20.0
EPS = 1e-5
GT = 86
SHUF = [(i + 16) % 32 for i in range(32)]



def _perm64():
    p = np.zeros(64, np.int64)
    for i in range(64):
        if i < 16:
            f, s = i, 0
        elif i < 32:
            f, s = i - 16, 1
        elif i < 48:
            f, s = i - 16, 0
        else:
            f, s = i - 32, 1
        p[i] = 2 * f + s
    return p


def _pan(w):
    O = w.shape[0]
    return np.ascontiguousarray(w.reshape(O // P, P, D).transpose(0, 2, 1))


def _rope_tabs(tok):
    invf = 10000.0 ** (-(2 * np.arange(32, dtype=np.float64)) / 64.0)
    Ct = np.zeros((P, T), np.float64)
    St = np.zeros((P, T), np.float64)
    for p in range(P):
        pp = p % 64
        if pp < 16:
            f, xi = pp, 0
        elif pp < 32:
            f, xi = pp - 16, 1
        elif pp < 48:
            f, xi = pp - 16, 0
        else:
            f, xi = pp - 32, 1
        ang = tok.astype(np.float64) * invf[f]
        Ct[p] = np.cos(ang)
        St[p] = np.sin(ang) * (1.0 if xi else -1.0)
    return Ct.astype(np.float32), St.astype(np.float32)


def _tile_lists():
    A = [(8 * a + kk, a, kk) for a in range(4) for kk in range(4)]
    Bl = []
    for j in range(32):
        R, tt = j // 8, j % 8
        a = R if tt < 4 else 7 - R
        Bl.append((j, a, tt % 4))
    return A, Bl


def _thr_table(r):
    A, Bl = _tile_lists()
    k = np.arange(P)
    cols = []
    for qa, lst in ((r, A), (7 - r, Bl)):
        for (j, a, kk) in lst:
            t = 128 * kk + 512 * (a - qa) + k
            for h in (0, 1):
                tl = t - 256 * h
                tl = np.where(tl <= 0, 0, np.where(tl >= 256, 1000, tl))
                cols.append(tl)
    return np.ascontiguousarray(np.stack(cols, axis=1).astype(np.float32))


def _ghost_tiles(g, u, d, c):
    out = np.empty((3 * GT, P, P), np.int32)
    n = 0
    for t in range(GT * c, GT * (c + 1)):
        ip, ht = t // 16, t % 16
        out[n] = g[128 * ip:128 * ip + 128, 128 * ht:128 * ht + 128].T
        n += 1
    for t in range(GT * c, GT * (c + 1)):
        ip, ht = t // 16, t % 16
        out[n] = u[128 * ip:128 * ip + 128, 128 * ht:128 * ht + 128].T
        n += 1
    for t in range(GT * c, GT * (c + 1)):
        ob, it = t // 43, t % 43
        out[n] = d[128 * ob:128 * ob + 128, 128 * it:128 * it + 128].T
        n += 1
    return np.ascontiguousarray(out)


def host_prep(inp):
    x = np.asarray(inp["L0_attn_a_hidden_states"], np.float32)
    w_qkv = np.asarray(inp["L0_attn_a_w_qkv"], np.float32)
    w_o = np.asarray(inp["L0_attn_a_w_o"], np.float32)
    wq = np.asarray(inp["L0_attn_b_wq"], np.float32)
    wk = np.asarray(inp["L0_attn_b_wk"], np.float32)
    wv = np.asarray(inp["L0_attn_b_wv"], np.float32)
    wout = np.asarray(inp["L0_attn_b_wout"], np.float32)
    lut = np.asarray(inp["L0_ffn_lut"], np.float32).reshape(1, 16)
    gidx = np.asarray(inp["L0_ffn_gate_idx"], np.int32)
    uidx = np.asarray(inp["L0_ffn_up_idx"], np.int32)
    didx = np.asarray(inp["L0_ffn_down_idx"], np.int32)

    perm = _perm64()
    permD = np.concatenate([64 * hh + perm for hh in range(32)])

    shared = {
        "wqkA": _pan(w_qkv[0:4096]),
        "wvA": np.ascontiguousarray(w_qkv[4096:6144].T),
        "woA": _pan(w_o),
        "wqB": _pan(wq[permD]),
        "wkB": _pan(wk[permD]),
        "wvB": np.ascontiguousarray(wv.T),
        "woB": _pan(wout),
        "lut": lut,
        "lamv": np.concatenate([
            np.asarray(inp["L0_attn_b_lambda_q1"], np.float32),
            np.asarray(inp["L0_attn_b_lambda_k1"], np.float32),
            np.asarray(inp["L0_attn_b_lambda_q2"], np.float32),
            np.asarray(inp["L0_attn_b_lambda_k2"], np.float32),
        ]).reshape(1, 256),
        "sublnw": np.asarray(inp["L0_attn_b_subln_weight"], np.float32).reshape(128, 1),
        "iota": np.ascontiguousarray(
            np.tile(np.arange(256, dtype=np.float32), (P, 1)).astype(ml_dtypes.bfloat16)),
    }

    in_maps, toks = [], []
    for c in range(8):
        g, r = c // 4, c % 4
        tok = np.r_[512 * r:512 * r + 512, 512 * (7 - r):512 * (8 - r)]
        toks.append((g, tok))
        Ct, St = _rope_tabs(tok)
        m = dict(shared)
        m["xin"] = np.ascontiguousarray(x[g][tok].T)
        m["ctab"] = np.ascontiguousarray(np.concatenate([Ct, St], axis=1))
        m["thr"] = _thr_table(r)
        m["bidx"] = _ghost_tiles(gidx, uidx, didx, c)
        in_maps.append(m)
    return in_maps, toks



class Sem:
    def __init__(self, h):
        self.h = h
        self.n = 0


def build_nc(upto="out"):
    nc = bass.Bass("TRN2", target_bir_lowering=False)
    es = ExitStack()

    di = lambda name, shape, dt: nc.dram_tensor(name, shape, dt, kind="ExternalInput").ap()
    d_xin = di("xin", [D, T], F32)
    d_wqkA = di("wqkA", [32, D, P], F32)
    d_wvA = di("wvA", [D, D], F32)
    d_woA = di("woA", [16, D, P], F32)
    d_wqB = di("wqB", [16, D, P], F32)
    d_wkB = di("wkB", [16, D, P], F32)
    d_wvB = di("wvB", [D, D], F32)
    d_woB = di("woB", [16, D, P], F32)
    d_lut = di("lut", [1, 16], F32)
    d_lamv = di("lamv", [1, 256], F32)
    d_sublnw = di("sublnw", [P, 1], F32)
    d_iota = di("iota", [P, 256], BF16)
    d_ctab = di("ctab", [P, 2048], F32)
    d_thr = di("thr", [P, 96], F32)
    d_bidx = di("bidx", [3 * GT, P, P], I32)
    d_out = nc.dram_tensor("out", [D, T], F32, kind="ExternalOutput").ap()
    DBG = (upto == "dbg")
    if DBG:
        d_dlnb = nc.dram_tensor("dlnb", [P, NCH * T], BF16, kind="ExternalOutput").ap()
        d_dkh = nc.dram_tensor("dkh", [P, 4096], BF16, kind="ExternalOutput").ap()
        d_dvh = nc.dram_tensor("dvh", [P, 4096], BF16, kind="ExternalOutput").ap()
        d_dqh = nc.dram_tensor("dqh", [P, T], BF16, kind="ExternalOutput").ap()
        d_de = nc.dram_tensor("de", [P, 2048], BF16, kind="ExternalOutput").ap()
        d_dao = nc.dram_tensor("dao", [P, 8 * T], BF16, kind="ExternalOutput").ap()
    if upto == "dbgb":
        d_dkh = nc.dram_tensor("dkh", [P, 4096], BF16, kind="ExternalOutput").ap()
        d_dvh = nc.dram_tensor("dvh", [P, 4096], BF16, kind="ExternalOutput").ap()
        d_dqh = nc.dram_tensor("dqh", [P, T], BF16, kind="ExternalOutput").ap()
        d_dao = nc.dram_tensor("dao", [P, 8 * T], BF16, kind="ExternalOutput").ap()
        d_drcp = nc.dram_tensor("drcp", [P, T], F32, kind="ExternalOutput").ap()
        d_dsst = nc.dram_tensor("dsst", [P, 512], F32, kind="ExternalOutput").ap()
        d_dusq = nc.dram_tensor("dusq", [P, 512], BF16, kind="ExternalOutput").ap()
        d_duac = nc.dram_tensor("duac", [P, 512], F32, kind="ExternalOutput").ap()
    DBGF = (upto == "dbgf")
    DBGB = (upto == "dbgb")
    if DBGF:
        d_dln2 = nc.dram_tensor("dln2", [P, NCH * T], BF16, kind="ExternalOutput").ap()
        d_dh = nc.dram_tensor("dh", [P, IC * 512], BF16, kind="ExternalOutput").ap()
        d_dxs = nc.dram_tensor("dxs", [P, NCH * T], F32, kind="ExternalOutput").ap()
        d_dhsg = nc.dram_tensor("dhsg", [P, 1024], BF16, kind="ExternalOutput").ap()
        d_dgsl = nc.dram_tensor("dgsl", [P, 2048], BF16, kind="ExternalOutput").ap()
        d_dusl = nc.dram_tensor("dusl", [P, 2048], BF16, kind="ExternalOutput").ap()
        d_dpgu = nc.dram_tensor("dpgu", [P, 2048], F32, kind="ExternalOutput").ap()

    d_gsl = [nc.dram_tensor(f"gsl{i}", [GT * P, P], BF16).ap() for i in range(3)]
    d_gw = [nc.dram_tensor(f"gw{i}", [8 * GT * P, P], BF16).ap() for i in range(3)]
    d_kvi = [nc.dram_tensor(f"kvi{i}", [4096, T], BF16).ap() for i in range(2)]
    d_kvg = [nc.dram_tensor(f"kvg{i}", [4 * 4096, T], BF16).ap() for i in range(2)]

    streams = {k: [] for k in "tvsgy"}
    mksem = lambda name: Sem(es.enter_context(nc.semaphore(name)))
    sY, sW, sG, sT, sV, sS, sC = (mksem(k) for k in
                                  ("sY", "sW", "sG", "sT", "sV", "sS", "sC"))
    sYi = mksem("sYi")
    sBL = mksem("sBL")
    sBS = [mksem(f"sBS{i}") for i in range(3)]
    sCg = mksem("sCg")
    sKS = mksem("sKS")
    sKV = mksem("sKV")
    sSl = [mksem(f"sSl{i}") for i in range(2)]
    sVw = [mksem(f"sVw{i}") for i in range(2)]
    sFg = [mksem("sFg0"), mksem("sFg1")]
    sFd = mksem("sFd")
    sOut = mksem("sOut")
    sX = mksem("sX")
    ENG_SEM = {"t": sT, "v": sV, "s": sS, "g": sG, "y": sY}

    def em(e, f):
        streams[e].append(f)

    def wait(e, sem, val=None):
        v = sem.n if val is None else val
        if v > 0:
            em(e, lambda E, s=sem.h, vv=v: E.wait_ge(s, vv))

    def _inc(i, sem, k):
        if sem is not None:
            i.then_inc(sem.h, k)
        return i

    def mm(out, lhsT, rhs, start, stop, sem=None, tp=None):
        if sem:
            sem.n += 1
        em('t', lambda E, o=out, l=lhsT, r=rhs, a=start, b=stop, s=sem, t=tp:
           _inc(E.matmul(o, l, r, start=a, stop=b, tile_position=t), s, 1))

    def act(out, in_, fn, bias=None, scale=None, sem=None):
        if sem:
            sem.n += 1
        kw = {}
        if bias is not None:
            kw["bias"] = bias
        if scale is not None:
            kw["scale"] = scale
        em('s', lambda E, o=out, i=in_, f=fn, kw=dict(kw), s=sem:
           _inc(E.activation(o, i, f, **kw), s, 1))

    def v_op(name, *args, sem=None):
        if sem:
            sem.n += 1
        em('v', lambda E, n=name, a=args, s=sem: _inc(getattr(E, n)(*a), s, 1))

    def dma(q, out, in_, sem):
        sem.n += 16
        em(q, lambda E, o=out, i=in_, s=sem: _inc(E.dma_start(out=o, in_=i), s, 16))

    def v_delay(cycles=256):
        em('v', lambda E, c=cycles: E.nop(cycle_cnt=c, nofuse=True))

    def t_pause(cycles=256):
        em('t', lambda E, c=cycles: E.nop(cycle_cnt=c, nofuse=True))

    def s_flush():
        em('s', lambda E: E.activation(flt[0:1, 0:1], cst[0:1, 0:1], AF.Copy))

    def v_flush():
        em('v', lambda E: E.memset(flt[0:1, 4:6], 0.0))

    def barrier():
        marks = {}
        for e in "tvsgy":
            s = ENG_SEM[e]
            s.n += 1
            em(e, lambda E, ss=s: _inc(E.nop(), ss, 1))
            marks[e] = s.n
        for e in "tvsgy":
            for o in "tvsgy":
                if o != e:
                    wait(e, ENG_SEM[o], marks[o])

    sb = lambda name, shape, dt: es.enter_context(nc.sbuf_tensor(name, shape, dt))
    xs = sb("xs", [P, NCH * T], F32)
    ctab = sb("ctab_s", [P, 2048], F32)
    thr = sb("thr_s", [P, 96], F32)
    iota = sb("iotab", [P, 256], BF16)
    onesb = sb("onesb", [P, P], BF16)
    ones1f = sb("ones1f", [1, P], F32)
    cst = sb("cst", [P, 2], F32)
    lutr = sb("lutr", [1, 16], F32)
    lutbc = sb("lutbc", [P, 16], F32)
    lam_bc = sb("lam_bc", [P, 1], F32)
    sublnw = sb("sublnw_s", [P, 1], F32)
    xb = [sb(f"xb{i}", [P, T], BF16) for i in range(2)]
    sq = [sb("sqb0", [P, T], BF16)] * 2
    mt = sb("mt", [P, T], F32)
    vt = sb("vt", [P, T], F32)
    rt = sb("rt", [P, T], F32)
    s4 = sb("s4", [P, T], F32)
    kvst = [sb("kvst0", [P, T], BF16)] * 2
    flt = sb("flt", [1, 8], F32)

    nc.const_aps.register(F32, SHIFT_B, cst[:, 0:1])
    nc.const_aps.register(F32, EPS, cst[:, 1:2])

    v_op("memset", onesb[:, :], 1.0)
    v_op("memset", ones1f[:, :], 1.0)
    v_op("memset", cst[:, 0:1], SHIFT_B)
    v_op("memset", cst[:, 1:2], EPS, sem=sV)
    ini_v = sV.n

    dma('y', xs[:, :].rearrange("p (c t) -> p c t", c=NCH),
        d_xin.rearrange("(c p) t -> p c t", p=P), sYi)
    dma('y', ctab[:, :], d_ctab, sYi)
    dma('y', thr[:, :], d_thr, sYi)
    dma('y', iota[:, :], d_iota, sYi)
    dma('y', lutr[:, :], d_lut, sYi)
    dma('y', sublnw[:, :], d_sublnw, sYi)
    ini_y = sYi.n

    wait('s', sV, ini_v)
    wait('s', sYi, ini_y)

    pbc_ctx = ExitStack()
    p_bc = pbc_ctx.enter_context(nc.psum_tensor("p_bc", [P, 16], F32))
    wait('t', sV, ini_v)
    wait('t', sYi, ini_y)
    mm(p_bc[:, :], ones1f[:, :], lutr[:, :], True, True, sem=sT)
    wait('v', sT)
    v_delay()
    v_op("tensor_copy", lutbc[:, :], p_bc[:, :], sem=sV)
    lut_v = sV.n
    pbc_ctx.close()

    bctx = ExitStack()
    g_idx = [bctx.enter_context(nc.sbuf_tensor(f"gidx{i}", [P, 4 * P], I32, side="right")) for i in range(2)]
    g_idxb = bctx.enter_context(nc.sbuf_tensor("gidxb", [P, 4 * P], BF16, side="right"))
    g_acc = bctx.enter_context(nc.sbuf_tensor("gacc", [P, 4 * P], BF16, side="right"))
    g_tsj = bctx.enter_context(nc.sbuf_tensor("gtsj", [P, 4 * P], BF16, side="right"))

    groups = []
    for w in range(3):
        t0 = 0
        while t0 < GT:
            n = min(4, GT - t0)
            groups.append((w, t0, n))
            t0 += n

    ag_mark = [None, None, None]
    idx_free = [0, 0]

    def _gen():
        for gi, (w, t0, n) in enumerate(groups):
            buf, F = gi % 2, n * P
            yield ("load", (w, t0, n, buf, F))
            yield ("cast", (buf, F))
            for j in range(16):
                yield ("ts", (j, F))
                if j > 0:
                    yield ("tt", F)
            yield ("store", (w, t0, n, F))
            if (gi + 1) % 22 == 0:
                yield ("ag", w)

    bgen = _gen()
    bdone = [False]
    bclosed = [False]
    bload = [0]

    def pull_build(k):
        if bdone[0]:
            return
        for _ in range(k):
            try:
                kind, a = next(bgen)
            except StopIteration:
                bdone[0] = True
                return
            if kind == "load":
                w, t0, n, buf, F = a
                wait('y', sV, idx_free[buf])
                src = d_bidx[w * GT + t0:w * GT + t0 + n, :, :].rearrange("n p m -> p n m")
                dma('y', g_idx[buf][:, 0:F].rearrange("p (n m) -> p n m", n=n), src, sBL)
                bload[0] = sBL.n
            elif kind == "cast":
                buf, F = a
                wait('v', sBL, bload[0])
                wait('v', sV, lut_v)
                v_op("tensor_copy", g_idxb[:, 0:F], g_idx[buf][:, 0:F], sem=sV)
                idx_free[buf] = sV.n
            elif kind == "ts":
                j, F = a
                dst = g_acc if j == 0 else g_tsj
                v_op("tensor_scalar", dst[:, 0:F], g_idxb[:, 0:F], float(j),
                     lutbc[:, j:j + 1], ALU.is_equal, ALU.mult)
            elif kind == "tt":
                F = a
                v_op("tensor_add", g_acc[:, 0:F], g_acc[:, 0:F], g_tsj[:, 0:F])
            elif kind == "store":
                w, t0, n, F = a
                v_op("tensor_copy", g_tsj[:, 0:128], g_acc[:, 0:128], sem=sV)
                v_flush()
                wait('y', sV)
                dst = d_gsl[w][t0 * P:(t0 + n) * P, :].rearrange("(n p) m -> p n m", p=P)
                dma('y', dst, g_acc[:, 0:F].rearrange("p (n m) -> p n m", n=n), sBS[w])
            elif kind == "ag":
                w = a
                wait('g', sBS[w], sBS[w].n)
                sCg.n += 1
                em('g', lambda E, i=d_gsl[w], o=d_gw[w], s=sCg: _inc(
                    E.collective_compute("AllGather", ALU.bypass,
                                         replica_groups=[list(range(8))],
                                         ins=[i.opt()], outs=[o.opt()]), s, 1))
                ag_mark[w] = sCg.n

    ln_ctr = [0]

    def ln_phase(lnb, x_ready):
        xr_sem, xr_val = x_ready
        ln_ctr[0] += 1
        pctx = ExitStack()
        pdm = pctx.enter_context(nc.psum_tensor(f"pdm{ln_ctr[0]}", [P, T], F32))
        pdq = pctx.enter_context(nc.psum_tensor(f"pdq{ln_ctr[0]}", [P, T], F32))
        wait('s', xr_sem, xr_val)
        stat_mark = [0, 0]
        for hc in range(NCH):
            b = hc % 2
            if hc >= 2:
                wait('s', sT, stat_mark[b])
            act(xb[b][:, :], xs[:, T * hc:T * hc + T], AF.Copy, sem=sS)
            s_flush()
            wait('v', sS)
            v_delay()
            wait('v', sT, max(stat_mark))
            v_op("tensor_mul", sq[b][:, :], xb[b][:, :], xb[b][:, :], sem=sV)
            v_flush()
            wait('t', sV)
            t_pause()
            for qt in range(2):
                mm(pdm[:, 512 * qt:512 * qt + 512], onesb[:, :],
                   xb[b][:, 512 * qt:512 * qt + 512], hc == 0, hc == NCH - 1)
            for qt in range(2):
                mm(pdq[:, 512 * qt:512 * qt + 512], onesb[:, :],
                   sq[b][:, 512 * qt:512 * qt + 512], hc == 0, hc == NCH - 1,
                   sem=(sT if qt == 1 else None))
            stat_mark[b] = sT.n
        wait('v', sT)
        v_op("tensor_scalar_mul", mt[:, :], pdm[:, :], 1.0 / D)
        v_op("tensor_scalar_mul", vt[:, :], pdq[:, :], 1.0 / D)
        v_op("tensor_mul", rt[:, :], mt[:, :], mt[:, :])
        v_op("tensor_sub", vt[:, :], vt[:, :], rt[:, :], sem=sV)
        wait('s', sV)
        act(s4[:, :], vt[:, :], AF.Ln, bias=EPS)
        act(rt[:, :], s4[:, :], AF.Exp, scale=-0.5, sem=sS)
        s_flush()
        wait('v', sS)
        v_delay()
        for hc in range(NCH):
            v_op("tensor_sub", vt[:, :], xs[:, T * hc:T * hc + T], mt[:, :])
            v_op("tensor_mul", lnb[:, T * hc:T * hc + T], vt[:, :], rt[:, :],
                 sem=(sV if hc == NCH - 1 else None))
        v_flush()
        pctx.close()
        return sV.n

    slab_mark = {}

    def proj_fm(pq, pans, pidx, slab, slab_sem, rhs, rhs_rdy, pulls=0):
        key = slab.name if hasattr(slab, "name") else id(slab)
        wait('g', sT, slab_mark.get(key, 0))
        src = pans[pidx, :, :].rearrange("(c p) m -> p c m", p=P)
        dma('g', slab[:, :].rearrange("p (c m) -> p c m", c=NCH), src, slab_sem)
        ldv = slab_sem.n
        pull_build(pulls)
        wait('t', slab_sem, ldv)
        wait('t', rhs_rdy[0], rhs_rdy[1])
        t_pause()
        for hc in range(NCH):
            for qt in range(2):
                mm(pq[:, 512 * qt:512 * qt + 512], slab[:, P * hc:P * hc + P],
                   rhs[:, T * hc + 512 * qt:T * hc + 512 * qt + 512],
                   hc == 0, hc == NCH - 1,
                   sem=(sT if (hc == NCH - 1 and qt == 1) else None))
        slab_mark[key] = sT.n

    def attention(which, lnb, ln_rdy):
        a = which == 0
        kv_i, kv_g = d_kvi[which], d_kvg[which]
        kv6i = kv_i.rearrange("(kv tt p two) c -> kv tt two p c", kv=2, tt=8, p=P, two=2)
        kv6g = kv_g.rearrange("(r kv tt p two) c -> r kv two p tt c", r=4, kv=2, tt=8, p=P, two=2)
        AList, BList = _tile_lists()

        at = ExitStack()
        asb = lambda name, shape, dt: at.enter_context(nc.sbuf_tensor(name + str(which), shape, dt))
        qh = [asb(f"qh{i}", [P, T], BF16) for i in range(2)]
        Kh = asb("Kh", [P, 4096], BF16)
        Vh = asb("Vh", [P, 4096], BF16)
        esl = [asb(f"esl{i}", [P, 2048], BF16) for i in range(2)]
        wsl = [asb(f"wsl{i}", [P, 2048], BF16) for i in range(2)]
        ao8 = asb("ao8", [P, 8 * T], BF16)
        rcp = asb("rcp", [P, T], F32)
        uacc = asb("uacc", [P, 512], F32)
        ub = asb("ub", [P, 512], F32)
        usq = asb("usq", [P, T], BF16)
        sst = asb("sst", [P, T], F32)
        vw = [asb(f"vw{i}", [P, 512], BF16) for i in range(2)]
        vst = [asb("vstg0", [P, 512], BF16)] * 2

        p1 = ExitStack()
        pk = p1.enter_context(nc.psum_tensor(f"pk{which}", [P, T], F32))
        if not a:
            plam = p1.enter_context(nc.psum_tensor("plam", [P, 1], F32))
            lamv = at.enter_context(nc.sbuf_tensor("lamv_s", [1, 256], F32))
            dma('y', lamv[:, :], d_lamv, sYi)
            wait('v', sYi, sYi.n)
            v_op("tensor_mul", mt[0:1, 0:64], lamv[0:1, 0:64], lamv[0:1, 64:128])
            v_op("tensor_mul", mt[0:1, 64:128], lamv[0:1, 128:192], lamv[0:1, 192:256])
            v_op("reduce_sum", vt[0:1, 0:1], mt[0:1, 0:64], AX.X)
            v_op("reduce_sum", vt[0:1, 1:2], mt[0:1, 64:128], AX.X, sem=sV)
            wait('s', sV)
            act(s4[0:1, 0:2], vt[0:1, 0:2], AF.Exp, sem=sS)
            s_flush()
            wait('v', sS)
            v_delay()
            v_op("tensor_sub", vt[0:1, 0:1], s4[0:1, 0:1], s4[0:1, 1:2])
            v_op("tensor_scalar_add", vt[0:1, 0:1], vt[0:1, 0:1], LAMBDA_INIT, sem=sV)
            wait('t', sV)
            t_pause()
            mm(plam[:, :], ones1f[:, :], vt[0:1, 0:1], True, True, sem=sT)
            wait('v', sT)
            v_delay()
            v_op("tensor_copy", lam_bc[:, :], plam[:, :], sem=sV)

        kst_mark = [0, 0]
        for hc in range(NCH):
            pans = d_wqkA if a else d_wkB
            proj_fm(pk, pans, (16 + hc) if a else hc, wsl[hc % 2],
                    sSl[hc % 2], lnb, (sV, ln_rdy), pulls=3)
            b2 = hc % 2
            wait('s', sT)
            wait('s', sKS, max(kst_mark))
            if a:
                act(kvst[b2][:, :], pk[:, :], AF.Copy, sem=sS)
                s_flush()
                kdone, keng = sS.n, sS
            else:
                wait('s', sV)
                act(mt[:, :], pk[:, :], AF.Copy, sem=sS)
                s_flush()
                wait('v', sS)
                v_delay()
                wait('v', sKS, max(kst_mark))
                v_op("stream_shuffle", vt[:, :], mt[:, :], SHUF)
                v_op("tensor_mul", s4[:, :], mt[:, :], ctab[:, 0:T])
                v_op("tensor_mul", vt[:, :], vt[:, :], ctab[:, T:2 * T])
                v_op("tensor_add", kvst[b2][:, :], s4[:, :], vt[:, :], sem=sV)
                v_flush()
                kdone, keng = sV.n, sV
            wait('t', keng, kdone)
            wait('y', keng, kdone)
            dma('y', kv_i[P * hc:P * hc + P, :], kvst[b2][:, :], sKS)
            kst_mark[b2] = sKS.n
        p1.close()

        p2 = ExitStack()
        pv = p2.enter_context(nc.psum_tensor(f"pv{which}", [P, 8 * 512], F32))
        vsrc = d_wvA if a else d_wvB
        vst_mark = [0, 0]
        vw_mark = [0, 0]
        evac_mark = 0
        for dvc in range(4):
            for hc in range(NCH):
                b2 = hc % 2
                wait('g', sT, vw_mark[b2])
                dma('g', vw[b2][:, :], vsrc[P * hc:P * hc + P, 512 * dvc:512 * dvc + 512], sVw[b2])
                pull_build(1)
                wait('t', sVw[b2], sVw[b2].n)
                if dvc == 0 and hc == 0:
                    wait('t', sV, ln_rdy)
                    t_pause()
                    wait('t', sS)
                if hc == 0:
                    wait('t', sS, evac_mark)
                for tt in range(8):
                    mm(pv[:, 512 * tt:512 * tt + 512],
                       lnb[:, T * hc + P * tt:T * hc + P * tt + P],
                       vw[b2][:, :], hc == 0, hc == NCH - 1,
                       sem=(sT if tt == 7 else None))
                vw_mark[b2] = sT.n
            wait('s', sT)
            for tt in range(8):
                b3 = tt % 2
                wait('s', sKS, max(vst_mark))
                act(vst[b3][:, :], pv[:, 512 * tt:512 * tt + 512], AF.Copy, sem=sS)
                s_flush()
                wait('y', sS)
                dst = kv6i[1, tt, dvc // 2, :, 512 * (dvc % 2):512 * (dvc % 2) + 512]
                dma('y', dst, vst[b3][:, :], sKS)
                vst_mark[b3] = sKS.n
            evac_mark = sS.n
        vevac_mark = sS.n
        p2.close()

        wait('g', sKS, sKS.n)
        sC.n += 1
        em('g', lambda E, i=kv_i, o=kv_g, s=sC: _inc(
            E.collective_compute("AllGather", ALU.bypass,
                                 replica_groups=[[0, 1, 2, 3], [4, 5, 6, 7]],
                                 ins=[i.opt()], outs=[o.opt()]), s, 1))
        kv_ready = sC.n

        p3 = ExitStack()
        pscore = p3.enter_context(nc.psum_tensor(f"ps{which}", [P, 2048], F32))
        pav = p3.enter_context(nc.psum_tensor(f"pav{which}", [P, 512], F32))
        pd = p3.enter_context(nc.psum_tensor(f"pd{which}", [P, T], F32))
        pav2 = p3.enter_context(nc.psum_tensor(f"pav2{which}", [P, 512], F32)) if not a else None

        head_end = [0]
        ao_free = [0]
        epi_v = [0]

        def out_proj(group):
            pans = d_woA if a else d_woB
            wait('t', sV)
            t_pause()
            for oc in range(NCH):
                b2 = oc % 2
                key = wsl[b2].name
                wait('g', sT, slab_mark.get(key, 0))
                src = pans[oc, :, :].rearrange("(c p) m -> p c m", p=P)
                dma('g', wsl[b2][:, :].rearrange("p (c m) -> p c m", c=NCH), src, sSl[b2])
                wait('t', sSl[b2], sSl[b2].n)
                for qt in range(2):
                    for h in range(8):
                        hc = group * 8 + h
                        mm(pscore[:, 512 * qt:512 * qt + 512],
                           wsl[b2][:, P * hc:P * hc + P],
                           ao8[:, T * h + 512 * qt:T * h + 512 * qt + 512],
                           h == 0, h == 7, sem=(sT if h == 7 else None))
                slab_mark[key] = sT.n
                wait('v', sT)
                v_delay()
                for qt in range(2):
                    v_op("tensor_add", xs[:, T * oc + 512 * qt:T * oc + 512 * qt + 512],
                         xs[:, T * oc + 512 * qt:T * oc + 512 * qt + 512],
                         pscore[:, 512 * qt:512 * qt + 512],
                         sem=(sV if qt == 1 else None))
                v_flush()
                wait('t', sV)
            ao_free[0] = sT.n

        for hc in range(NHEAD):
            hb = hc % 2
            wait('y', sC, kv_ready)
            wait('y', sT, head_end[0])
            for R in range(4):
                dma('y', Kh[:, 1024 * R:1024 * R + 1024],
                    kv_g[4096 * R + P * hc:4096 * R + P * hc + P, :], sKV)
            for R in range(4):
                src = kv6g[R, 1, hc // 8, :, :, P * (hc % 8):P * (hc % 8) + P]
                dma('y', Vh[:, 1024 * R:1024 * R + 1024]
                    .rearrange("p (tt m) -> p tt m", tt=8), src, sKV)
            kvld = sKV.n

            pq = pscore[:, 0:T]
            if hc == 0:
                wait('t', sS, vevac_mark)
            proj_fm(pq, d_wqkA if a else d_wqB, hc, wsl[hb], sSl[hb],
                    lnb, (sV, ln_rdy), pulls=2)
            wait('s', sT)
            if a:
                act(qh[hb][:, :], pq, AF.Copy, sem=sS)
                s_flush()
                qrdy, qeng = sS.n, sS
            else:
                wait('s', sV)
                act(mt[:, :], pq, AF.Copy, sem=sS)
                s_flush()
                wait('v', sS)
                v_delay()
                v_op("stream_shuffle", vt[:, :], mt[:, :], SHUF)
                v_op("tensor_mul", s4[:, :], mt[:, :], ctab[:, 0:T])
                v_op("tensor_mul", vt[:, :], vt[:, :], ctab[:, T:2 * T])
                v_op("tensor_add", qh[hb][:, :], s4[:, :], vt[:, :], sem=sV)
                v_flush()
                qrdy, qeng = sV.n, sV
            wait('t', qeng, qrdy)
            t_pause()

            phases = [(0, AList, 0), (1, BList, 16)] if a else \
                     [(0, list(range(32)), None), (1, list(range(32)), None)]
            for (qw, tiles, thrbase) in phases:
                q0 = 512 * qw
                per = 4 if a else 2
                nblk = len(tiles) // per
                started = [False]
                exp_mark = {}
                for blk in range(nblk):
                    bt = tiles[per * blk:per * blk + per]
                    eb = esl[blk % 2]
                    if blk == 0:
                        wait('t', sKV, kvld)
                        wait('t', sV, epi_v[0])
                        t_pause()
                    if blk >= 1:
                        wait('t', sS, exp_mark[blk - 1])
                    for i2, tile in enumerate(bt):
                        j = tile[0] if a else tile
                        if a:
                            mm(pscore[:, 512 * i2:512 * i2 + 512],
                               Kh[:, P * j:P * j + P], qh[hb][:, q0:q0 + 512],
                               True, True, sem=(sT if i2 == per - 1 else None))
                        else:
                            mm(pscore[:, 1024 * i2:1024 * i2 + 512],
                               Kh[0:64, P * j:P * j + P], qh[hb][0:64, q0:q0 + 512],
                               True, True, tp=(0, 0))
                            mm(pscore[:, 1024 * i2 + 512:1024 * i2 + 1024],
                               Kh[64:P, P * j:P * j + P], qh[hb][64:P, q0:q0 + 512],
                               True, True, tp=(64, 0),
                               sem=(sT if i2 == per - 1 else None))
                    wait('s', sT)
                    act(eb[:, :], pscore[:, :], AF.Exp,
                        bias=(None if a else SHIFT_B),
                        scale=(SCALE_A if a else SCALE_B), sem=sS)
                    s_flush()
                    exp_mark[blk] = sS.n
                    if a:
                        wait('v', sS)
                        v_delay()
                        for i2, (j, aa, kk) in enumerate(bt):
                            tg = thrbase + per * blk + i2
                            for h2 in range(2):
                                c0 = 512 * i2 + 256 * h2
                                v_op("scalar_tensor_tensor", eb[:, c0:c0 + 256],
                                     iota[:, :], thr[:, 2 * tg + h2:2 * tg + h2 + 1],
                                     eb[:, c0:c0 + 256], ALU.is_ge, ALU.mult,
                                     sem=(sV if (i2 == per - 1 and h2 == 1) else None))
                        v_flush()
                        wait('t', sV)
                        t_pause(160)
                        pull_build(4)
                    else:
                        wait('t', sS)
                        t_pause(160)
                        pull_build(2)
                    last = blk == nblk - 1
                    st0 = not started[0]
                    started[0] = True
                    for i2, tile in enumerate(bt):
                        j = tile[0] if a else tile
                        fin = last and i2 == per - 1
                        if a:
                            mm(pd[:, q0:q0 + 512], onesb[:, :],
                               eb[:, 512 * i2:512 * i2 + 512], st0 and i2 == 0, fin)
                            mm(pav[:, :], Vh[:, P * j:P * j + P],
                               eb[:, 512 * i2:512 * i2 + 512], st0 and i2 == 0, fin,
                               sem=(sT if fin else None))
                        else:
                            s00 = st0 and i2 == 0
                            mm(pd[:, 0:512], onesb[:, :],
                               eb[:, 1024 * i2:1024 * i2 + 512], s00, fin)
                            mm(pd[:, 512:1024], onesb[:, :],
                               eb[:, 1024 * i2 + 512:1024 * i2 + 1024], s00, fin)
                            mm(pav[:, :], Vh[:, P * j:P * j + P],
                               eb[:, 1024 * i2:1024 * i2 + 512], s00, fin)
                            mm(pav2[:, :], Vh[:, P * j:P * j + P],
                               eb[:, 1024 * i2 + 512:1024 * i2 + 1024], s00, fin,
                               sem=(sT if fin else None))
                wait('s', sT)
                if a:
                    act(s4[:, 0:512], pd[:, q0:q0 + 512], AF.Ln)
                    act(rcp[:, q0:q0 + 512], s4[:, 0:512], AF.Exp, scale=-1.0, sem=sS)
                    s_flush()
                    wait('v', sS)
                    v_delay()
                    if hc >= 8:
                        wait('v', sT, ao_free[0])
                    v_op("tensor_mul", ao8[:, T * (hc % 8) + q0:T * (hc % 8) + q0 + 512],
                         pav[:, :], rcp[:, q0:q0 + 512], sem=sV)
                    v_flush()
                    epi_v[0] = sV.n
                    wait('t', sV, epi_v[0])
                else:
                    act(s4[:, 0:T], pd[:, :], AF.Ln)
                    act(rcp[:, :], s4[:, 0:T], AF.Exp, scale=-1.0, sem=sS)
                    s_flush()
                    wait('v', sS)
                    v_delay()
                    v_op("tensor_mul", uacc[:, :], pav[:, :], rcp[:, 0:512])
                    v_op("tensor_mul", ub[:, :], pav2[:, :], rcp[:, 512:1024])
                    v_op("tensor_scalar_mul", ub[:, :], ub[:, :], lam_bc[:, 0:1])
                    v_op("tensor_sub", uacc[:, :], uacc[:, :], ub[:, :])
                    v_op("tensor_mul", usq[:, q0:q0 + 512], uacc[:, :], uacc[:, :], sem=sV)
                    v_flush()
                    wait('t', sV)
                    t_pause()
                    wait('t', sS, exp_mark[nblk - 1])
                    mm(pscore[:, 0:512], onesb[:, :], usq[:, q0:q0 + 512],
                       True, True, sem=sT)
                    wait('s', sT)
                    act(rt[:, 0:512], pscore[:, 0:512], AF.Ln, bias=EPS, scale=1.0 / P)
                    act(sst[:, q0:q0 + 512], rt[:, 0:512], AF.Exp, scale=-0.5, sem=sS)
                    sX.n += 1
                    em('s', lambda E, s=sX: _inc(
                        E.activation(flt[0:1, 2:3], cst[0:1, 0:1], AF.Copy), s, 1))
                    wait('v', sX, sX.n)
                    wait('v', sS)
                    v_delay()
                    if hc >= 8:
                        wait('v', sT, ao_free[0])
                    v_op("tensor_mul", ub[:, :], uacc[:, :], sst[:, q0:q0 + 512])
                    v_op("tensor_scalar_mul",
                         ao8[:, T * (hc % 8) + q0:T * (hc % 8) + q0 + 512],
                         ub[:, :], sublnw[:, 0:1], sem=sV)
                    v_flush()
                    epi_v[0] = sV.n
                    wait('t', sV, epi_v[0])
            head_end[0] = sT.n
            if DBGB and which == 1 and hc == 15:
                wait('y', sV)
                wait('y', sS)
                dma('y', d_dkh, Kh[:, :], sOut)
                dma('y', d_dvh, Vh[:, :], sOut)
                dma('y', d_dqh, qh[hb][:, :], sOut)
                dma('y', d_dao, ao8[:, :], sOut)
                dma('y', d_drcp, rcp[:, :], sOut)
                dma('y', d_dsst, sst[:, :], sOut)
                dma('y', d_dusq, usq[:, :], sOut)
                dma('y', d_duac, uacc[:, :], sOut)
                wait('y', sOut, sOut.n)
                break
            if DBG and hc == 1:
                wait('y', sV)
                wait('y', sS)
                dma('y', d_dlnb, lnb[:, :], sOut)
                dma('y', d_dkh, Kh[:, :], sOut)
                dma('y', d_dvh, Vh[:, :], sOut)
                dma('y', d_dqh, qh[hb][:, :], sOut)
                dma('y', d_de, esl[(len(tiles) // per - 1) % 2][:, :], sOut)
                dma('y', d_dao, ao8[:, :], sOut)
                wait('y', sOut, sOut.n)
                break
            if hc % 8 == 7:
                out_proj(hc // 8)

        p3.close()
        at.close()
        return sV.n

    def ffn(lnb, ln_rdy, fidx):
        ft = ExitStack()
        fsb = lambda name, shape, dt: ft.enter_context(nc.sbuf_tensor(name + str(fidx), shape, dt))
        hbuf = fsb("hbuf", [P, IC * 512], BF16)
        gsl = [fsb(f"gsl{i}", [P, 2048], BF16) for i in range(2)]
        usl = [fsb(f"usl{i}", [P, 2048], BF16) for i in range(2)]
        dsl = [fsb("dsl0", [P, 2816], BF16), fsb("dsl1", [P, 2688], BF16)]
        hsg = [fsb("hsg0", [P, 512], BF16)] * 2

        fp = ExitStack()
        pgu = [fp.enter_context(nc.psum_tensor(f"pgu{fidx}{i}", [P, T], F32)) for i in range(2)]
        pdn = [fp.enter_context(nc.psum_tensor(f"pdn{fidx}{i}", [P, 512], F32)) for i in range(2)]

        pull_build(10 ** 9)
        if not bclosed[0]:
            bctx.close()
            bclosed[0] = True
        g_rows = lambda w, ip: d_gw[w][2048 * ip:2048 * (ip + 1), :] \
            .rearrange("(c p) m -> p c m", p=P)

        gu_mark = [0, 0]
        pgu_mark = [0, 0]
        dn_mark = [0, 0]
        down_end = 0
        for half in range(2):
            q0 = 512 * half
            ic_order = list(range(IC)) if half == 0 else list(range(IC - 1, -1, -1))
            for icpos, ic in enumerate(ic_order):
                bb = icpos % 2
                if icpos == 0:
                    wait('y', sCg, ag_mark[0])
                    wait('y', sCg, ag_mark[1])
                wait('y', sT, gu_mark[bb])
                dma('y', gsl[bb][:, :].rearrange("p (c m) -> p c m", c=NCH),
                    g_rows(0, ic), sFg[bb])
                dma('y', usl[bb][:, :].rearrange("p (c m) -> p c m", c=NCH),
                    g_rows(1, ic), sFg[bb])
                ldv = sFg[bb].n
                wait('t', sFg[bb], ldv)
                if icpos == 0:
                    wait('t', sV, ln_rdy)
                    wait('t', sV)
                    wait('t', sS)
                    t_pause()
                wait('t', sV, pgu_mark[bb])
                for hc in range(NCH):
                    mm(pgu[bb][:, 0:512], gsl[bb][:, P * hc:P * hc + P],
                       lnb[:, T * hc + q0:T * hc + q0 + 512], hc == 0, hc == NCH - 1)
                for hc in range(NCH):
                    mm(pgu[bb][:, 512:1024], usl[bb][:, P * hc:P * hc + P],
                       lnb[:, T * hc + q0:T * hc + q0 + 512], hc == 0, hc == NCH - 1,
                       sem=(sT if hc == NCH - 1 else None))
                gu_mark[bb] = sT.n
                wait('s', sT)
                wait('s', sV, max(pgu_mark))
                act(hsg[bb][:, :], pgu[bb][:, 0:512], AF.Silu, sem=sS)
                s_flush()
                wait('v', sS)
                v_delay()
                v_op("tensor_mul", hbuf[:, 512 * ic:512 * ic + 512],
                     pgu[bb][:, 512:1024], hsg[bb][:, :], sem=sV)
                v_flush()
                pgu_mark[bb] = sV.n
            lic = ic_order[-1]
            v_op("tensor_copy", vt[:, 0:512], hbuf[:, 512 * lic:512 * lic + 512])
            v_op("tensor_copy", hbuf[:, 512 * lic:512 * lic + 512], vt[:, 0:512])
            v_op("tensor_copy", rt[:, 0:512], hbuf[:, 512 * lic:512 * lic + 512], sem=sV)
            v_flush()
            h_rdy = sV.n
            import os as _os
            if _os.environ.get("K_NO_DOWN") and half == 1:
                wait('y', sV, h_rdy)
                dma('y', d_dln2, lnb[:, :], sOut)
                dma('y', d_dh, hbuf[:, :], sOut)
                dma('y', d_dxs, xs[:, :], sOut)
                wait('y', sOut, sOut.n)
                break
            if DBGF and half == 1:
                wait('s', sV, h_rdy)
                act(mt[:, :], pgu[0][:, :], AF.Copy)
                act(vt[:, :], pgu[1][:, :], AF.Copy, sem=sS)
                wait('y', sS)
                wait('y', sV, h_rdy)
                dma('y', d_dln2, lnb[:, :], sOut)
                dma('y', d_dh, hbuf[:, :], sOut)
                dma('y', d_dxs, xs[:, :], sOut)
                dma('y', d_dhsg[:, 0:512], hsg[0][:, :], sOut)
                dma('y', d_dhsg[:, 512:1024], hsg[1][:, :], sOut)
                dma('y', d_dgsl, gsl[0][:, :], sOut)
                dma('y', d_dusl, usl[0][:, :], sOut)
                dma('y', d_dpgu[:, 0:1024], mt[:, :], sOut)
                dma('y', d_dpgu[:, 1024:2048], vt[:, :], sOut)
            for oc in range(NCH):
                if oc == 0:
                    wait('y', sCg, ag_mark[2])
                    wait('y', sV, h_rdy)
                for pc in range(2):
                    db = (oc * 2 + pc) % 2
                    wait('y', sT, dn_mark[db])
                    n_ic = 22 if pc == 0 else 21
                    rows = d_gw[2][5504 * oc + 2816 * pc:
                                   5504 * oc + 2816 * pc + 128 * n_ic, :] \
                        .rearrange("(c p) m -> p c m", p=P)
                    dma('y', dsl[db][:, 0:128 * n_ic]
                        .rearrange("p (c m) -> p c m", c=n_ic), rows, sFd)
                ldv = sFd.n
                if oc == 0:
                    wait('t', sV, h_rdy)
                    t_pause()
                wait('t', sFd, ldv)
                pb = oc % 2
                for dpos, ic in enumerate(ic_order):
                    pc = 0 if ic < 22 else 1
                    db = (oc * 2 + pc) % 2
                    col = P * (ic - 22 * pc)
                    mm(pdn[pb][:, :], dsl[db][:, col:col + P],
                       hbuf[:, 512 * ic:512 * ic + 512], dpos == 0, dpos == IC - 1,
                       sem=(sT if dpos == IC - 1 else None))
                dn_mark[0] = dn_mark[1] = sT.n
                wait('v', sT)
                v_delay()
                v_op("tensor_add", xs[:, T * oc + q0:T * oc + q0 + 512],
                     xs[:, T * oc + q0:T * oc + q0 + 512], pdn[pb][:, :], sem=sV)
                v_flush()
                wait('t', sV)
            down_end = sT.n
        fp.close()
        ft.close()
        return sV.n

    x_ready = (sYi, ini_y)
    stages = ["a", "f1", "b", "f2"] if upto == "out" else \
        {"x1": ["a"], "x2": ["a", "f1"], "x3": ["a", "f1", "b"], "dbg": ["a"],
         "dbgf": ["a", "f1"], "dbgb": ["a", "f1", "b"]}[upto]

    for st in stages:
        ctx = ExitStack()
        lnb = ctx.enter_context(nc.sbuf_tensor("lnb_" + st, [P, NCH * T], BF16))
        ln_rdy = ln_phase(lnb, x_ready)
        if st == "a":
            mark = attention(0, lnb, ln_rdy)
        elif st == "b":
            mark = attention(1, lnb, ln_rdy)
        elif st == "f1":
            mark = ffn(lnb, ln_rdy, 0)
        else:
            mark = ffn(lnb, ln_rdy, 1)
        x_ready = (sV, mark)
        ctx.close()
        barrier()

    wait('y', x_ready[0], x_ready[1])
    dma('y', d_out.rearrange("(c p) t -> p c t", p=P),
        xs[:, :].rearrange("p (c t) -> p c t", c=NCH), sOut)
    wait('y', sOut, sOut.n)

    with nc.Block() as block:
        @block.tensor
        def _(E):
            for f in streams['t']:
                f(E)

        @block.vector
        def _(E):
            for f in streams['v']:
                f(E)

        @block.scalar
        def _(E):
            for f in streams['s']:
                f(E)

        @block.gpsimd
        def _(E):
            for f in streams['g']:
                f(E)

        @block.sync
        def _(E):
            for f in streams['y']:
                f(E)

    n_inst = sum(len(v) for v in streams.values())
    print(f"[build] upto={upto} instructions ~{n_inst}")
    return nc


def kernel(**inputs):
    in_maps, toks = host_prep(inputs)
    nc = build_nc("out")
    res = run_bass_kernel_spmd(nc, in_maps, core_ids=list(range(8)))
    out = np.zeros((2, 4096, 2048), np.float32)
    for c, r in enumerate(res.results):
        g, tok = toks[c]
        out[g][tok] = np.asarray(r["out"]).T
    return out


# version 15 (final)
# speedup vs baseline: 1.1592x; 1.0270x over previous
import math
import numpy as np
import ml_dtypes
from contextlib import ExitStack

import concourse.bass as bass
import concourse.mybir as mybir
from concourse.bass_utils import run_bass_kernel_spmd

F32 = mybir.dt.float32
BF16 = mybir.dt.bfloat16
I32 = mybir.dt.int32
AF = mybir.ActivationFunctionType
ALU = mybir.AluOpType
AX = mybir.AxisListType

P = 128
D = 2048
T = 1024
NCH = D // P
NHEAD = 16
IC = 43
LAMBDA_INIT = 0.8 - 0.6 * math.exp(-0.3 * 12)
SCALE_A = 1.0 / math.sqrt(128.0)
SCALE_B = math.sqrt(128.0)
SHIFT_B = -20.0
EPS = 1e-5
GT = 86
SHUF = [(i + 16) % 32 for i in range(32)]



def _perm64():
    p = np.zeros(64, np.int64)
    for i in range(64):
        if i < 16:
            f, s = i, 0
        elif i < 32:
            f, s = i - 16, 1
        elif i < 48:
            f, s = i - 16, 0
        else:
            f, s = i - 32, 1
        p[i] = 2 * f + s
    return p


def _pan(w):
    O = w.shape[0]
    return np.ascontiguousarray(w.reshape(O // P, P, D).transpose(0, 2, 1))


def _rope_tabs(tok):
    invf = 10000.0 ** (-(2 * np.arange(32, dtype=np.float64)) / 64.0)
    Ct = np.zeros((P, T), np.float64)
    St = np.zeros((P, T), np.float64)
    for p in range(P):
        pp = p % 64
        if pp < 16:
            f, xi = pp, 0
        elif pp < 32:
            f, xi = pp - 16, 1
        elif pp < 48:
            f, xi = pp - 16, 0
        else:
            f, xi = pp - 32, 1
        ang = tok.astype(np.float64) * invf[f]
        Ct[p] = np.cos(ang)
        St[p] = np.sin(ang) * (1.0 if xi else -1.0)
    return Ct.astype(np.float32), St.astype(np.float32)


def _tile_lists():
    A = [(8 * a + kk, a, kk) for a in range(4) for kk in range(4)]
    Bl = []
    for j in range(32):
        R, tt = j // 8, j % 8
        a = R if tt < 4 else 7 - R
        Bl.append((j, a, tt % 4))
    return A, Bl


def _thr_table(r):
    A, Bl = _tile_lists()
    k = np.arange(P)
    cols = []
    for qa, lst in ((r, A), (7 - r, Bl)):
        for (j, a, kk) in lst:
            t = 128 * kk + 512 * (a - qa) + k
            for h in (0, 1):
                tl = t - 256 * h
                tl = np.where(tl <= 0, 0, np.where(tl >= 256, 1000, tl))
                cols.append(tl)
    return np.ascontiguousarray(np.stack(cols, axis=1).astype(np.float32))


def _ghost_tiles(g, u, d, c):
    out = np.empty((3 * GT, P, P), np.int32)
    n = 0
    for t in range(GT * c, GT * (c + 1)):
        ip, ht = t // 16, t % 16
        out[n] = g[128 * ip:128 * ip + 128, 128 * ht:128 * ht + 128].T
        n += 1
    for t in range(GT * c, GT * (c + 1)):
        ip, ht = t // 16, t % 16
        out[n] = u[128 * ip:128 * ip + 128, 128 * ht:128 * ht + 128].T
        n += 1
    for t in range(GT * c, GT * (c + 1)):
        ob, it = t // 43, t % 43
        out[n] = d[128 * ob:128 * ob + 128, 128 * it:128 * it + 128].T
        n += 1
    return np.ascontiguousarray(out)


def host_prep(inp):
    x = np.asarray(inp["L0_attn_a_hidden_states"], np.float32)
    w_qkv = np.asarray(inp["L0_attn_a_w_qkv"], np.float32)
    w_o = np.asarray(inp["L0_attn_a_w_o"], np.float32)
    wq = np.asarray(inp["L0_attn_b_wq"], np.float32)
    wk = np.asarray(inp["L0_attn_b_wk"], np.float32)
    wv = np.asarray(inp["L0_attn_b_wv"], np.float32)
    wout = np.asarray(inp["L0_attn_b_wout"], np.float32)
    lut = np.asarray(inp["L0_ffn_lut"], np.float32).reshape(1, 16)
    gidx = np.asarray(inp["L0_ffn_gate_idx"], np.int32)
    uidx = np.asarray(inp["L0_ffn_up_idx"], np.int32)
    didx = np.asarray(inp["L0_ffn_down_idx"], np.int32)

    perm = _perm64()
    permD = np.concatenate([64 * hh + perm for hh in range(32)])

    shared = {
        "wqkA": _pan(w_qkv[0:4096]),
        "wvA": np.ascontiguousarray(w_qkv[4096:6144].T),
        "woA": _pan(w_o),
        "wqB": _pan(wq[permD]),
        "wkB": _pan(wk[permD]),
        "wvB": np.ascontiguousarray(wv.T),
        "woB": _pan(wout),
        "lut": lut,
        "lamv": np.concatenate([
            np.asarray(inp["L0_attn_b_lambda_q1"], np.float32),
            np.asarray(inp["L0_attn_b_lambda_k1"], np.float32),
            np.asarray(inp["L0_attn_b_lambda_q2"], np.float32),
            np.asarray(inp["L0_attn_b_lambda_k2"], np.float32),
        ]).reshape(1, 256),
        "sublnw": np.asarray(inp["L0_attn_b_subln_weight"], np.float32).reshape(128, 1),
        "iota": np.ascontiguousarray(
            np.tile(np.arange(256, dtype=np.float32), (P, 1)).astype(ml_dtypes.bfloat16)),
    }

    in_maps, toks = [], []
    for c in range(8):
        g, r = c // 4, c % 4
        tok = np.r_[512 * r:512 * r + 512, 512 * (7 - r):512 * (8 - r)]
        toks.append((g, tok))
        Ct, St = _rope_tabs(tok)
        m = dict(shared)
        m["xin"] = np.ascontiguousarray(x[g][tok].T)
        m["ctab"] = np.ascontiguousarray(np.concatenate([Ct, St], axis=1))
        m["thr"] = _thr_table(r)
        m["bidx"] = _ghost_tiles(gidx, uidx, didx, c)
        in_maps.append(m)
    return in_maps, toks



class Sem:
    def __init__(self, h):
        self.h = h
        self.n = 0


def build_nc(upto="out"):
    nc = bass.Bass("TRN2", target_bir_lowering=False)
    es = ExitStack()

    di = lambda name, shape, dt: nc.dram_tensor(name, shape, dt, kind="ExternalInput").ap()
    d_xin = di("xin", [D, T], F32)
    d_wqkA = di("wqkA", [32, D, P], F32)
    d_wvA = di("wvA", [D, D], F32)
    d_woA = di("woA", [16, D, P], F32)
    d_wqB = di("wqB", [16, D, P], F32)
    d_wkB = di("wkB", [16, D, P], F32)
    d_wvB = di("wvB", [D, D], F32)
    d_woB = di("woB", [16, D, P], F32)
    d_lut = di("lut", [1, 16], F32)
    d_lamv = di("lamv", [1, 256], F32)
    d_sublnw = di("sublnw", [P, 1], F32)
    d_iota = di("iota", [P, 256], BF16)
    d_ctab = di("ctab", [P, 2048], F32)
    d_thr = di("thr", [P, 96], F32)
    d_bidx = di("bidx", [3 * GT, P, P], I32)
    d_out = nc.dram_tensor("out", [D, T], F32, kind="ExternalOutput").ap()
    DBG = (upto == "dbg")
    if DBG:
        d_dlnb = nc.dram_tensor("dlnb", [P, NCH * T], BF16, kind="ExternalOutput").ap()
        d_dkh = nc.dram_tensor("dkh", [P, 4096], BF16, kind="ExternalOutput").ap()
        d_dvh = nc.dram_tensor("dvh", [P, 4096], BF16, kind="ExternalOutput").ap()
        d_dqh = nc.dram_tensor("dqh", [P, T], BF16, kind="ExternalOutput").ap()
        d_de = nc.dram_tensor("de", [P, 2048], BF16, kind="ExternalOutput").ap()
        d_dao = nc.dram_tensor("dao", [P, 8 * T], BF16, kind="ExternalOutput").ap()
    if upto == "dbgb":
        d_dkh = nc.dram_tensor("dkh", [P, 4096], BF16, kind="ExternalOutput").ap()
        d_dvh = nc.dram_tensor("dvh", [P, 4096], BF16, kind="ExternalOutput").ap()
        d_dqh = nc.dram_tensor("dqh", [P, T], BF16, kind="ExternalOutput").ap()
        d_dao = nc.dram_tensor("dao", [P, 8 * T], BF16, kind="ExternalOutput").ap()
        d_drcp = nc.dram_tensor("drcp", [P, T], F32, kind="ExternalOutput").ap()
        d_dsst = nc.dram_tensor("dsst", [P, 512], F32, kind="ExternalOutput").ap()
        d_dusq = nc.dram_tensor("dusq", [P, 512], BF16, kind="ExternalOutput").ap()
        d_duac = nc.dram_tensor("duac", [P, 512], F32, kind="ExternalOutput").ap()
    DBGF = (upto == "dbgf")
    DBGB = (upto == "dbgb")
    if DBGF:
        d_dln2 = nc.dram_tensor("dln2", [P, NCH * T], BF16, kind="ExternalOutput").ap()
        d_dh = nc.dram_tensor("dh", [P, IC * 512], BF16, kind="ExternalOutput").ap()
        d_dxs = nc.dram_tensor("dxs", [P, NCH * T], F32, kind="ExternalOutput").ap()
        d_dhsg = nc.dram_tensor("dhsg", [P, 1024], BF16, kind="ExternalOutput").ap()
        d_dgsl = nc.dram_tensor("dgsl", [P, 2048], BF16, kind="ExternalOutput").ap()
        d_dusl = nc.dram_tensor("dusl", [P, 2048], BF16, kind="ExternalOutput").ap()
        d_dpgu = nc.dram_tensor("dpgu", [P, 2048], F32, kind="ExternalOutput").ap()

    d_gsl = [nc.dram_tensor(f"gsl{i}", [GT * P, P], BF16).ap() for i in range(3)]
    d_gw = [nc.dram_tensor(f"gw{i}", [8 * GT * P, P], BF16).ap() for i in range(3)]
    d_kvi = [nc.dram_tensor(f"kvi{i}", [4096, T], BF16).ap() for i in range(2)]
    d_kvg = [nc.dram_tensor(f"kvg{i}", [4 * 4096, T], BF16).ap() for i in range(2)]

    streams = {k: [] for k in "tvsgy"}
    mksem = lambda name: Sem(es.enter_context(nc.semaphore(name)))
    sY, sW, sG, sT, sV, sS, sC = (mksem(k) for k in
                                  ("sY", "sW", "sG", "sT", "sV", "sS", "sC"))
    sYi = mksem("sYi")
    sBL = mksem("sBL")
    sBS = [mksem(f"sBS{i}") for i in range(3)]
    sCg = mksem("sCg")
    sKS = mksem("sKS")
    sKV = mksem("sKV")
    sSl = [mksem(f"sSl{i}") for i in range(2)]
    sVw = [mksem(f"sVw{i}") for i in range(2)]
    sFg = [mksem("sFg0"), mksem("sFg1")]
    sFd = mksem("sFd")
    sOut = mksem("sOut")
    sX = mksem("sX")
    ENG_SEM = {"t": sT, "v": sV, "s": sS, "g": sG, "y": sY}

    def em(e, f):
        streams[e].append(f)

    def wait(e, sem, val=None):
        v = sem.n if val is None else val
        if v > 0:
            em(e, lambda E, s=sem.h, vv=v: E.wait_ge(s, vv))

    def _inc(i, sem, k):
        if sem is not None:
            i.then_inc(sem.h, k)
        return i

    def mm(out, lhsT, rhs, start, stop, sem=None, tp=None):
        if sem:
            sem.n += 1
        em('t', lambda E, o=out, l=lhsT, r=rhs, a=start, b=stop, s=sem, t=tp:
           _inc(E.matmul(o, l, r, start=a, stop=b, tile_position=t), s, 1))

    def act(out, in_, fn, bias=None, scale=None, sem=None):
        if sem:
            sem.n += 1
        kw = {}
        if bias is not None:
            kw["bias"] = bias
        if scale is not None:
            kw["scale"] = scale
        em('s', lambda E, o=out, i=in_, f=fn, kw=dict(kw), s=sem:
           _inc(E.activation(o, i, f, **kw), s, 1))

    def v_op(name, *args, sem=None):
        if sem:
            sem.n += 1
        em('v', lambda E, n=name, a=args, s=sem: _inc(getattr(E, n)(*a), s, 1))

    def dma(q, out, in_, sem):
        sem.n += 16
        em(q, lambda E, o=out, i=in_, s=sem: _inc(E.dma_start(out=o, in_=i), s, 16))

    def v_delay(cycles=256):
        em('v', lambda E, c=cycles: E.nop(cycle_cnt=c, nofuse=True))

    def t_pause(cycles=256):
        em('t', lambda E, c=cycles: E.nop(cycle_cnt=c, nofuse=True))

    def s_flush():
        em('s', lambda E: E.activation(flt[0:1, 0:1], cst[0:1, 0:1], AF.Copy))

    def v_flush():
        em('v', lambda E: E.memset(flt[0:1, 4:6], 0.0))

    def barrier():
        marks = {}
        for e in "tvsgy":
            s = ENG_SEM[e]
            s.n += 1
            em(e, lambda E, ss=s: _inc(E.nop(), ss, 1))
            marks[e] = s.n
        for e in "tvsgy":
            for o in "tvsgy":
                if o != e:
                    wait(e, ENG_SEM[o], marks[o])

    sb = lambda name, shape, dt: es.enter_context(nc.sbuf_tensor(name, shape, dt))
    xs = sb("xs", [P, NCH * T], F32)
    ctab = sb("ctab_s", [P, 2048], F32)
    thr = sb("thr_s", [P, 96], F32)
    iota = sb("iotab", [P, 256], BF16)
    onesb = sb("onesb", [P, P], BF16)
    ones1f = sb("ones1f", [1, P], F32)
    cst = sb("cst", [P, 2], F32)
    lutr = sb("lutr", [1, 16], F32)
    lutbc = sb("lutbc", [P, 16], F32)
    lam_bc = sb("lam_bc", [P, 1], F32)
    sublnw = sb("sublnw_s", [P, 1], F32)
    xb = [sb(f"xb{i}", [P, T], BF16) for i in range(2)]
    sq = [sb("sqb0", [P, T], BF16)] * 2
    mt = sb("mt", [P, T], F32)
    vt = sb("vt", [P, T], F32)
    rt = sb("rt", [P, T], F32)
    s4 = sb("s4", [P, T], F32)
    kvst = [sb("kvst0", [P, T], BF16)] * 2
    flt = sb("flt", [1, 8], F32)

    nc.const_aps.register(F32, SHIFT_B, cst[:, 0:1])
    nc.const_aps.register(F32, EPS, cst[:, 1:2])

    v_op("memset", onesb[:, :], 1.0)
    v_op("memset", ones1f[:, :], 1.0)
    v_op("memset", cst[:, 0:1], SHIFT_B)
    v_op("memset", cst[:, 1:2], EPS, sem=sV)
    ini_v = sV.n

    dma('y', xs[:, :].rearrange("p (c t) -> p c t", c=NCH),
        d_xin.rearrange("(c p) t -> p c t", p=P), sYi)
    dma('y', ctab[:, :], d_ctab, sYi)
    dma('y', thr[:, :], d_thr, sYi)
    dma('y', iota[:, :], d_iota, sYi)
    dma('y', lutr[:, :], d_lut, sYi)
    dma('y', sublnw[:, :], d_sublnw, sYi)
    ini_y = sYi.n

    wait('s', sV, ini_v)
    wait('s', sYi, ini_y)

    pbc_ctx = ExitStack()
    p_bc = pbc_ctx.enter_context(nc.psum_tensor("p_bc", [P, 16], F32))
    wait('t', sV, ini_v)
    wait('t', sYi, ini_y)
    mm(p_bc[:, :], ones1f[:, :], lutr[:, :], True, True, sem=sT)
    wait('v', sT)
    v_delay()
    v_op("tensor_copy", lutbc[:, :], p_bc[:, :], sem=sV)
    lut_v = sV.n
    pbc_ctx.close()

    bctx = ExitStack()
    g_idx = [bctx.enter_context(nc.sbuf_tensor(f"gidx{i}", [P, 4 * P], I32, side="right")) for i in range(2)]
    g_idxb = bctx.enter_context(nc.sbuf_tensor("gidxb", [P, 4 * P], BF16, side="right"))
    g_acc = bctx.enter_context(nc.sbuf_tensor("gacc", [P, 4 * P], BF16, side="right"))
    g_tsj = bctx.enter_context(nc.sbuf_tensor("gtsj", [P, 4 * P], BF16, side="right"))

    groups = []
    for w in range(3):
        t0 = 0
        while t0 < GT:
            n = min(4, GT - t0)
            groups.append((w, t0, n))
            t0 += n

    ag_mark = [None, None, None]
    idx_free = [0, 0]

    def _gen():
        for gi, (w, t0, n) in enumerate(groups):
            buf, F = gi % 2, n * P
            yield ("load", (w, t0, n, buf, F))
            yield ("cast", (buf, F))
            for j in range(16):
                yield ("ts", (j, F))
                if j > 0:
                    yield ("tt", F)
            yield ("store", (w, t0, n, F))
            if (gi + 1) % 22 == 0:
                yield ("ag", w)

    bgen = _gen()
    bdone = [False]
    bclosed = [False]
    bload = [0]

    def pull_build(k):
        if bdone[0]:
            return
        for _ in range(k):
            try:
                kind, a = next(bgen)
            except StopIteration:
                bdone[0] = True
                return
            if kind == "load":
                w, t0, n, buf, F = a
                wait('y', sV, idx_free[buf])
                src = d_bidx[w * GT + t0:w * GT + t0 + n, :, :].rearrange("n p m -> p n m")
                dma('y', g_idx[buf][:, 0:F].rearrange("p (n m) -> p n m", n=n), src, sBL)
                bload[0] = sBL.n
            elif kind == "cast":
                buf, F = a
                wait('v', sBL, bload[0])
                wait('v', sV, lut_v)
                v_op("tensor_copy", g_idxb[:, 0:F], g_idx[buf][:, 0:F], sem=sV)
                idx_free[buf] = sV.n
            elif kind == "ts":
                j, F = a
                dst = g_acc if j == 0 else g_tsj
                v_op("tensor_scalar", dst[:, 0:F], g_idxb[:, 0:F], float(j),
                     lutbc[:, j:j + 1], ALU.is_equal, ALU.mult)
            elif kind == "tt":
                F = a
                v_op("tensor_add", g_acc[:, 0:F], g_acc[:, 0:F], g_tsj[:, 0:F])
            elif kind == "store":
                w, t0, n, F = a
                v_op("tensor_copy", g_tsj[:, 0:128], g_acc[:, 0:128], sem=sV)
                v_flush()
                wait('y', sV)
                dst = d_gsl[w][t0 * P:(t0 + n) * P, :].rearrange("(n p) m -> p n m", p=P)
                dma('y', dst, g_acc[:, 0:F].rearrange("p (n m) -> p n m", n=n), sBS[w])
            elif kind == "ag":
                w = a
                wait('g', sBS[w], sBS[w].n)
                sCg.n += 1
                em('g', lambda E, i=d_gsl[w], o=d_gw[w], s=sCg: _inc(
                    E.collective_compute("AllGather", ALU.bypass,
                                         replica_groups=[list(range(8))],
                                         ins=[i.opt()], outs=[o.opt()]), s, 1))
                ag_mark[w] = sCg.n

    ln_ctr = [0]

    def ln_phase(lnb, x_ready):
        xr_sem, xr_val = x_ready
        ln_ctr[0] += 1
        pctx = ExitStack()
        pdm = pctx.enter_context(nc.psum_tensor(f"pdm{ln_ctr[0]}", [P, T], F32))
        pdq = pctx.enter_context(nc.psum_tensor(f"pdq{ln_ctr[0]}", [P, T], F32))
        wait('s', xr_sem, xr_val)
        stat_mark = [0, 0]
        for hc in range(NCH):
            b = hc % 2
            if hc >= 2:
                wait('s', sT, stat_mark[b])
            act(xb[b][:, :], xs[:, T * hc:T * hc + T], AF.Copy, sem=sS)
            s_flush()
            wait('v', sS)
            v_delay()
            wait('v', sT, max(stat_mark))
            v_op("tensor_mul", sq[b][:, :], xb[b][:, :], xb[b][:, :], sem=sV)
            v_flush()
            wait('t', sV)
            t_pause()
            for qt in range(2):
                mm(pdm[:, 512 * qt:512 * qt + 512], onesb[:, :],
                   xb[b][:, 512 * qt:512 * qt + 512], hc == 0, hc == NCH - 1)
            for qt in range(2):
                mm(pdq[:, 512 * qt:512 * qt + 512], onesb[:, :],
                   sq[b][:, 512 * qt:512 * qt + 512], hc == 0, hc == NCH - 1,
                   sem=(sT if qt == 1 else None))
            stat_mark[b] = sT.n
        wait('v', sT)
        v_op("tensor_scalar_mul", mt[:, :], pdm[:, :], 1.0 / D)
        v_op("tensor_scalar_mul", vt[:, :], pdq[:, :], 1.0 / D)
        v_op("tensor_mul", rt[:, :], mt[:, :], mt[:, :])
        v_op("tensor_sub", vt[:, :], vt[:, :], rt[:, :], sem=sV)
        wait('s', sV)
        act(s4[:, :], vt[:, :], AF.Ln, bias=EPS)
        act(rt[:, :], s4[:, :], AF.Exp, scale=-0.5, sem=sS)
        s_flush()
        wait('v', sS)
        v_delay()
        for hc in range(NCH):
            v_op("tensor_sub", vt[:, :], xs[:, T * hc:T * hc + T], mt[:, :])
            v_op("tensor_mul", lnb[:, T * hc:T * hc + T], vt[:, :], rt[:, :],
                 sem=(sV if hc == NCH - 1 else None))
        v_flush()
        pctx.close()
        return sV.n

    slab_mark = {}

    def proj_fm(pq, pans, pidx, slab, slab_sem, rhs, rhs_rdy, pulls=0):
        key = slab.name if hasattr(slab, "name") else id(slab)
        wait('g', sT, slab_mark.get(key, 0))
        src = pans[pidx, :, :].rearrange("(c p) m -> p c m", p=P)
        dma('g', slab[:, :].rearrange("p (c m) -> p c m", c=NCH), src, slab_sem)
        ldv = slab_sem.n
        pull_build(pulls)
        wait('t', slab_sem, ldv)
        wait('t', rhs_rdy[0], rhs_rdy[1])
        t_pause()
        for hc in range(NCH):
            for qt in range(2):
                mm(pq[:, 512 * qt:512 * qt + 512], slab[:, P * hc:P * hc + P],
                   rhs[:, T * hc + 512 * qt:T * hc + 512 * qt + 512],
                   hc == 0, hc == NCH - 1,
                   sem=(sT if (hc == NCH - 1 and qt == 1) else None))
        slab_mark[key] = sT.n

    def attention(which, lnb, ln_rdy):
        a = which == 0
        kv_i, kv_g = d_kvi[which], d_kvg[which]
        kv6i = kv_i.rearrange("(kv tt p two) c -> kv tt two p c", kv=2, tt=8, p=P, two=2)
        kv6g = kv_g.rearrange("(r kv tt p two) c -> r kv two p tt c", r=4, kv=2, tt=8, p=P, two=2)
        AList, BList = _tile_lists()

        at = ExitStack()
        asb = lambda name, shape, dt: at.enter_context(nc.sbuf_tensor(name + str(which), shape, dt))
        qh = [asb(f"qh{i}", [P, T], BF16) for i in range(2)]
        Kh = asb("Kh", [P, 4096], BF16)
        Vh = asb("Vh", [P, 4096], BF16)
        esl = [asb(f"esl{i}", [P, 2048], BF16) for i in range(2)]
        wsl = [asb(f"wsl{i}", [P, 2048], BF16) for i in range(2)]
        ao8 = asb("ao8", [P, 8 * T], BF16)
        rcp = asb("rcp", [P, T], F32)
        uacc = asb("uacc", [P, 512], F32)
        ub = asb("ub", [P, 512], F32)
        usq = asb("usq", [P, T], BF16)
        sst = asb("sst", [P, T], F32)
        vw = [asb(f"vw{i}", [P, 512], BF16) for i in range(2)]
        vst = [asb("vstg0", [P, 512], BF16)] * 2

        p1 = ExitStack()
        pk = p1.enter_context(nc.psum_tensor(f"pk{which}", [P, T], F32))
        if not a:
            plam = p1.enter_context(nc.psum_tensor("plam", [P, 1], F32))
            lamv = at.enter_context(nc.sbuf_tensor("lamv_s", [1, 256], F32))
            dma('y', lamv[:, :], d_lamv, sYi)
            wait('v', sYi, sYi.n)
            v_op("tensor_mul", mt[0:1, 0:64], lamv[0:1, 0:64], lamv[0:1, 64:128])
            v_op("tensor_mul", mt[0:1, 64:128], lamv[0:1, 128:192], lamv[0:1, 192:256])
            v_op("reduce_sum", vt[0:1, 0:1], mt[0:1, 0:64], AX.X)
            v_op("reduce_sum", vt[0:1, 1:2], mt[0:1, 64:128], AX.X, sem=sV)
            wait('s', sV)
            act(s4[0:1, 0:2], vt[0:1, 0:2], AF.Exp, sem=sS)
            s_flush()
            wait('v', sS)
            v_delay()
            v_op("tensor_sub", vt[0:1, 0:1], s4[0:1, 0:1], s4[0:1, 1:2])
            v_op("tensor_scalar_add", vt[0:1, 0:1], vt[0:1, 0:1], LAMBDA_INIT, sem=sV)
            wait('t', sV)
            t_pause()
            mm(plam[:, :], ones1f[:, :], vt[0:1, 0:1], True, True, sem=sT)
            wait('v', sT)
            v_delay()
            v_op("tensor_copy", lam_bc[:, :], plam[:, :], sem=sV)

        kst_mark = [0, 0]
        for hc in range(NCH):
            pans = d_wqkA if a else d_wkB
            proj_fm(pk, pans, (16 + hc) if a else hc, wsl[hc % 2],
                    sSl[hc % 2], lnb, (sV, ln_rdy), pulls=5)
            b2 = hc % 2
            wait('s', sT)
            wait('s', sKS, max(kst_mark))
            if a:
                act(kvst[b2][:, :], pk[:, :], AF.Copy, sem=sS)
                s_flush()
                kdone, keng = sS.n, sS
            else:
                wait('s', sV)
                act(mt[:, :], pk[:, :], AF.Copy, sem=sS)
                s_flush()
                wait('v', sS)
                v_delay()
                wait('v', sKS, max(kst_mark))
                v_op("stream_shuffle", vt[:, :], mt[:, :], SHUF)
                v_op("tensor_mul", s4[:, :], mt[:, :], ctab[:, 0:T])
                v_op("tensor_mul", vt[:, :], vt[:, :], ctab[:, T:2 * T])
                v_op("tensor_add", kvst[b2][:, :], s4[:, :], vt[:, :], sem=sV)
                v_flush()
                kdone, keng = sV.n, sV
            wait('t', keng, kdone)
            wait('y', keng, kdone)
            dma('y', kv_i[P * hc:P * hc + P, :], kvst[b2][:, :], sKS)
            kst_mark[b2] = sKS.n
        p1.close()

        p2 = ExitStack()
        pv = p2.enter_context(nc.psum_tensor(f"pv{which}", [P, 8 * 512], F32))
        vsrc = d_wvA if a else d_wvB
        vst_mark = [0, 0]
        vw_mark = [0, 0]
        evac_mark = 0
        for dvc in range(4):
            for hc in range(NCH):
                b2 = hc % 2
                wait('g', sT, vw_mark[b2])
                dma('g', vw[b2][:, :], vsrc[P * hc:P * hc + P, 512 * dvc:512 * dvc + 512], sVw[b2])
                pull_build(2)
                wait('t', sVw[b2], sVw[b2].n)
                if dvc == 0 and hc == 0:
                    wait('t', sV, ln_rdy)
                    t_pause()
                    wait('t', sS)
                if hc == 0:
                    wait('t', sS, evac_mark)
                for tt in range(8):
                    mm(pv[:, 512 * tt:512 * tt + 512],
                       lnb[:, T * hc + P * tt:T * hc + P * tt + P],
                       vw[b2][:, :], hc == 0, hc == NCH - 1,
                       sem=(sT if tt == 7 else None))
                vw_mark[b2] = sT.n
            wait('s', sT)
            for tt in range(8):
                b3 = tt % 2
                wait('s', sKS, max(vst_mark))
                act(vst[b3][:, :], pv[:, 512 * tt:512 * tt + 512], AF.Copy, sem=sS)
                s_flush()
                wait('y', sS)
                dst = kv6i[1, tt, dvc // 2, :, 512 * (dvc % 2):512 * (dvc % 2) + 512]
                dma('y', dst, vst[b3][:, :], sKS)
                vst_mark[b3] = sKS.n
            evac_mark = sS.n
        vevac_mark = sS.n
        p2.close()

        wait('g', sKS, sKS.n)
        sC.n += 1
        em('g', lambda E, i=kv_i, o=kv_g, s=sC: _inc(
            E.collective_compute("AllGather", ALU.bypass,
                                 replica_groups=[[0, 1, 2, 3], [4, 5, 6, 7]],
                                 ins=[i.opt()], outs=[o.opt()]), s, 1))
        kv_ready = sC.n

        p3 = ExitStack()
        pscore = p3.enter_context(nc.psum_tensor(f"ps{which}", [P, 2048], F32))
        pav = p3.enter_context(nc.psum_tensor(f"pav{which}", [P, 512], F32))
        pd = p3.enter_context(nc.psum_tensor(f"pd{which}", [P, T], F32))
        pav2 = p3.enter_context(nc.psum_tensor(f"pav2{which}", [P, 512], F32)) if not a else None

        head_end = [0]
        ao_free = [0]
        epi_v = [0]

        def out_proj(group):
            pans = d_woA if a else d_woB
            wait('t', sV)
            t_pause()
            for oc in range(NCH):
                b2 = oc % 2
                key = wsl[b2].name
                wait('g', sT, slab_mark.get(key, 0))
                src = pans[oc, :, :].rearrange("(c p) m -> p c m", p=P)
                dma('g', wsl[b2][:, :].rearrange("p (c m) -> p c m", c=NCH), src, sSl[b2])
                wait('t', sSl[b2], sSl[b2].n)
                for qt in range(2):
                    for h in range(8):
                        hc = group * 8 + h
                        mm(pscore[:, 512 * qt:512 * qt + 512],
                           wsl[b2][:, P * hc:P * hc + P],
                           ao8[:, T * h + 512 * qt:T * h + 512 * qt + 512],
                           h == 0, h == 7, sem=(sT if h == 7 else None))
                slab_mark[key] = sT.n
                wait('v', sT)
                v_delay()
                for qt in range(2):
                    v_op("tensor_add", xs[:, T * oc + 512 * qt:T * oc + 512 * qt + 512],
                         xs[:, T * oc + 512 * qt:T * oc + 512 * qt + 512],
                         pscore[:, 512 * qt:512 * qt + 512],
                         sem=(sV if qt == 1 else None))
                v_flush()
                wait('t', sV)
            ao_free[0] = sT.n

        for hc in range(NHEAD):
            hb = hc % 2
            wait('y', sC, kv_ready)
            wait('y', sT, head_end[0])
            for R in range(4):
                dma('y', Kh[:, 1024 * R:1024 * R + 1024],
                    kv_g[4096 * R + P * hc:4096 * R + P * hc + P, :], sKV)
            for R in range(4):
                src = kv6g[R, 1, hc // 8, :, :, P * (hc % 8):P * (hc % 8) + P]
                dma('y', Vh[:, 1024 * R:1024 * R + 1024]
                    .rearrange("p (tt m) -> p tt m", tt=8), src, sKV)
            kvld = sKV.n

            pq = pscore[:, 0:T]
            if hc == 0:
                wait('t', sS, vevac_mark)
            proj_fm(pq, d_wqkA if a else d_wqB, hc, wsl[hb], sSl[hb],
                    lnb, (sV, ln_rdy), pulls=2)
            wait('s', sT)
            if a:
                act(qh[hb][:, :], pq, AF.Copy, sem=sS)
                s_flush()
                qrdy, qeng = sS.n, sS
            else:
                wait('s', sV)
                act(mt[:, :], pq, AF.Copy, sem=sS)
                s_flush()
                wait('v', sS)
                v_delay()
                v_op("stream_shuffle", vt[:, :], mt[:, :], SHUF)
                v_op("tensor_mul", s4[:, :], mt[:, :], ctab[:, 0:T])
                v_op("tensor_mul", vt[:, :], vt[:, :], ctab[:, T:2 * T])
                v_op("tensor_add", qh[hb][:, :], s4[:, :], vt[:, :], sem=sV)
                v_flush()
                qrdy, qeng = sV.n, sV
            wait('t', qeng, qrdy)
            t_pause()

            phases = [(0, AList, 0), (1, BList, 16)] if a else \
                     [(0, list(range(32)), None), (1, list(range(32)), None)]
            for (qw, tiles, thrbase) in phases:
                q0 = 512 * qw
                per = 4 if a else 2
                nblk = len(tiles) // per
                started = [False]
                exp_mark = {}
                for blk in range(nblk):
                    bt = tiles[per * blk:per * blk + per]
                    eb = esl[blk % 2]
                    if blk == 0:
                        wait('t', sKV, kvld)
                        wait('t', sV, epi_v[0])
                        t_pause()
                    if blk >= 1:
                        wait('t', sS, exp_mark[blk - 1])
                    for i2, tile in enumerate(bt):
                        j = tile[0] if a else tile
                        if a:
                            mm(pscore[:, 512 * i2:512 * i2 + 512],
                               Kh[:, P * j:P * j + P], qh[hb][:, q0:q0 + 512],
                               True, True, sem=(sT if i2 == per - 1 else None))
                        else:
                            mm(pscore[:, 1024 * i2:1024 * i2 + 512],
                               Kh[0:64, P * j:P * j + P], qh[hb][0:64, q0:q0 + 512],
                               True, True, tp=(0, 0))
                            mm(pscore[:, 1024 * i2 + 512:1024 * i2 + 1024],
                               Kh[64:P, P * j:P * j + P], qh[hb][64:P, q0:q0 + 512],
                               True, True, tp=(64, 0),
                               sem=(sT if i2 == per - 1 else None))
                    wait('s', sT)
                    act(eb[:, :], pscore[:, :], AF.Exp,
                        bias=(None if a else SHIFT_B),
                        scale=(SCALE_A if a else SCALE_B), sem=sS)
                    s_flush()
                    exp_mark[blk] = sS.n
                    if a:
                        wait('v', sS)
                        v_delay()
                        for i2, (j, aa, kk) in enumerate(bt):
                            tg = thrbase + per * blk + i2
                            for h2 in range(2):
                                c0 = 512 * i2 + 256 * h2
                                v_op("scalar_tensor_tensor", eb[:, c0:c0 + 256],
                                     iota[:, :], thr[:, 2 * tg + h2:2 * tg + h2 + 1],
                                     eb[:, c0:c0 + 256], ALU.is_ge, ALU.mult,
                                     sem=(sV if (i2 == per - 1 and h2 == 1) else None))
                        v_flush()
                        wait('t', sV)
                        t_pause(160)
                        pull_build(8)
                    else:
                        wait('t', sS)
                        t_pause(160)
                        pull_build(4)
                    last = blk == nblk - 1
                    st0 = not started[0]
                    started[0] = True
                    for i2, tile in enumerate(bt):
                        j = tile[0] if a else tile
                        fin = last and i2 == per - 1
                        if a:
                            mm(pd[:, q0:q0 + 512], onesb[:, :],
                               eb[:, 512 * i2:512 * i2 + 512], st0 and i2 == 0, fin)
                            mm(pav[:, :], Vh[:, P * j:P * j + P],
                               eb[:, 512 * i2:512 * i2 + 512], st0 and i2 == 0, fin,
                               sem=(sT if fin else None))
                        else:
                            s00 = st0 and i2 == 0
                            mm(pd[:, 0:512], onesb[:, :],
                               eb[:, 1024 * i2:1024 * i2 + 512], s00, fin)
                            mm(pd[:, 512:1024], onesb[:, :],
                               eb[:, 1024 * i2 + 512:1024 * i2 + 1024], s00, fin)
                            mm(pav[:, :], Vh[:, P * j:P * j + P],
                               eb[:, 1024 * i2:1024 * i2 + 512], s00, fin)
                            mm(pav2[:, :], Vh[:, P * j:P * j + P],
                               eb[:, 1024 * i2 + 512:1024 * i2 + 1024], s00, fin,
                               sem=(sT if fin else None))
                wait('s', sT)
                if a:
                    act(s4[:, 0:512], pd[:, q0:q0 + 512], AF.Ln)
                    act(rcp[:, q0:q0 + 512], s4[:, 0:512], AF.Exp, scale=-1.0, sem=sS)
                    s_flush()
                    wait('v', sS)
                    v_delay()
                    if hc >= 8:
                        wait('v', sT, ao_free[0])
                    v_op("tensor_mul", ao8[:, T * (hc % 8) + q0:T * (hc % 8) + q0 + 512],
                         pav[:, :], rcp[:, q0:q0 + 512], sem=sV)
                    v_flush()
                    epi_v[0] = sV.n
                    wait('t', sV, epi_v[0])
                else:
                    act(s4[:, 0:T], pd[:, :], AF.Ln)
                    act(rcp[:, :], s4[:, 0:T], AF.Exp, scale=-1.0, sem=sS)
                    s_flush()
                    wait('v', sS)
                    v_delay()
                    v_op("tensor_mul", uacc[:, :], pav[:, :], rcp[:, 0:512])
                    v_op("tensor_mul", ub[:, :], pav2[:, :], rcp[:, 512:1024])
                    v_op("tensor_scalar_mul", ub[:, :], ub[:, :], lam_bc[:, 0:1])
                    v_op("tensor_sub", uacc[:, :], uacc[:, :], ub[:, :])
                    v_op("tensor_mul", usq[:, q0:q0 + 512], uacc[:, :], uacc[:, :], sem=sV)
                    v_flush()
                    wait('t', sV)
                    t_pause()
                    wait('t', sS, exp_mark[nblk - 1])
                    mm(pscore[:, 0:512], onesb[:, :], usq[:, q0:q0 + 512],
                       True, True, sem=sT)
                    wait('s', sT)
                    act(rt[:, 0:512], pscore[:, 0:512], AF.Ln, bias=EPS, scale=1.0 / P)
                    act(sst[:, q0:q0 + 512], rt[:, 0:512], AF.Exp, scale=-0.5, sem=sS)
                    sX.n += 1
                    em('s', lambda E, s=sX: _inc(
                        E.activation(flt[0:1, 2:3], cst[0:1, 0:1], AF.Copy), s, 1))
                    wait('v', sX, sX.n)
                    wait('v', sS)
                    v_delay()
                    if hc >= 8:
                        wait('v', sT, ao_free[0])
                    v_op("tensor_mul", ub[:, :], uacc[:, :], sst[:, q0:q0 + 512])
                    v_op("tensor_scalar_mul",
                         ao8[:, T * (hc % 8) + q0:T * (hc % 8) + q0 + 512],
                         ub[:, :], sublnw[:, 0:1], sem=sV)
                    v_flush()
                    epi_v[0] = sV.n
                    wait('t', sV, epi_v[0])
            head_end[0] = sT.n
            if DBGB and which == 1 and hc == 15:
                wait('y', sV)
                wait('y', sS)
                dma('y', d_dkh, Kh[:, :], sOut)
                dma('y', d_dvh, Vh[:, :], sOut)
                dma('y', d_dqh, qh[hb][:, :], sOut)
                dma('y', d_dao, ao8[:, :], sOut)
                dma('y', d_drcp, rcp[:, :], sOut)
                dma('y', d_dsst, sst[:, :], sOut)
                dma('y', d_dusq, usq[:, :], sOut)
                dma('y', d_duac, uacc[:, :], sOut)
                wait('y', sOut, sOut.n)
                break
            if DBG and hc == 1:
                wait('y', sV)
                wait('y', sS)
                dma('y', d_dlnb, lnb[:, :], sOut)
                dma('y', d_dkh, Kh[:, :], sOut)
                dma('y', d_dvh, Vh[:, :], sOut)
                dma('y', d_dqh, qh[hb][:, :], sOut)
                dma('y', d_de, esl[(len(tiles) // per - 1) % 2][:, :], sOut)
                dma('y', d_dao, ao8[:, :], sOut)
                wait('y', sOut, sOut.n)
                break
            if hc % 8 == 7:
                out_proj(hc // 8)

        p3.close()
        at.close()
        return sV.n

    def ffn(lnb, ln_rdy, fidx):
        ft = ExitStack()
        fsb = lambda name, shape, dt: ft.enter_context(nc.sbuf_tensor(name + str(fidx), shape, dt))
        hbuf = fsb("hbuf", [P, IC * 512], BF16)
        gsl = [fsb(f"gsl{i}", [P, 2048], BF16) for i in range(2)]
        usl = [fsb(f"usl{i}", [P, 2048], BF16) for i in range(2)]
        dsl = [fsb("dsl0", [P, 2816], BF16), fsb("dsl1", [P, 2688], BF16)]
        hsg = [fsb("hsg0", [P, 512], BF16)] * 2

        fp = ExitStack()
        pgu = [fp.enter_context(nc.psum_tensor(f"pgu{fidx}{i}", [P, T], F32)) for i in range(2)]
        pdn = [fp.enter_context(nc.psum_tensor(f"pdn{fidx}{i}", [P, 512], F32)) for i in range(2)]

        pull_build(10 ** 9)
        if not bclosed[0]:
            bctx.close()
            bclosed[0] = True
        g_rows = lambda w, ip: d_gw[w][2048 * ip:2048 * (ip + 1), :] \
            .rearrange("(c p) m -> p c m", p=P)

        gu_mark = [0, 0]
        pgu_mark = [0, 0]
        dn_mark = [0, 0]
        down_end = 0
        for half in range(2):
            q0 = 512 * half
            ic_order = list(range(IC)) if half == 0 else list(range(IC - 1, -1, -1))
            for icpos, ic in enumerate(ic_order):
                bb = icpos % 2
                if icpos == 0:
                    wait('y', sCg, ag_mark[0])
                    wait('y', sCg, ag_mark[1])
                wait('y', sT, gu_mark[bb])
                dma('y', gsl[bb][:, :].rearrange("p (c m) -> p c m", c=NCH),
                    g_rows(0, ic), sFg[bb])
                dma('y', usl[bb][:, :].rearrange("p (c m) -> p c m", c=NCH),
                    g_rows(1, ic), sFg[bb])
                ldv = sFg[bb].n
                wait('t', sFg[bb], ldv)
                if icpos == 0:
                    wait('t', sV, ln_rdy)
                    wait('t', sV)
                    wait('t', sS)
                    t_pause()
                wait('t', sV, pgu_mark[bb])
                for hc in range(NCH):
                    mm(pgu[bb][:, 0:512], gsl[bb][:, P * hc:P * hc + P],
                       lnb[:, T * hc + q0:T * hc + q0 + 512], hc == 0, hc == NCH - 1)
                for hc in range(NCH):
                    mm(pgu[bb][:, 512:1024], usl[bb][:, P * hc:P * hc + P],
                       lnb[:, T * hc + q0:T * hc + q0 + 512], hc == 0, hc == NCH - 1,
                       sem=(sT if hc == NCH - 1 else None))
                gu_mark[bb] = sT.n
                wait('s', sT)
                wait('s', sV, max(pgu_mark))
                act(hsg[bb][:, :], pgu[bb][:, 0:512], AF.Silu, sem=sS)
                s_flush()
                wait('v', sS)
                v_delay()
                v_op("tensor_mul", hbuf[:, 512 * ic:512 * ic + 512],
                     pgu[bb][:, 512:1024], hsg[bb][:, :], sem=sV)
                v_flush()
                pgu_mark[bb] = sV.n
            lic = ic_order[-1]
            v_op("tensor_copy", vt[:, 0:512], hbuf[:, 512 * lic:512 * lic + 512])
            v_op("tensor_copy", hbuf[:, 512 * lic:512 * lic + 512], vt[:, 0:512])
            v_op("tensor_copy", rt[:, 0:512], hbuf[:, 512 * lic:512 * lic + 512], sem=sV)
            v_flush()
            h_rdy = sV.n
            import os as _os
            if _os.environ.get("K_NO_DOWN") and half == 1:
                wait('y', sV, h_rdy)
                dma('y', d_dln2, lnb[:, :], sOut)
                dma('y', d_dh, hbuf[:, :], sOut)
                dma('y', d_dxs, xs[:, :], sOut)
                wait('y', sOut, sOut.n)
                break
            if DBGF and half == 1:
                wait('s', sV, h_rdy)
                act(mt[:, :], pgu[0][:, :], AF.Copy)
                act(vt[:, :], pgu[1][:, :], AF.Copy, sem=sS)
                wait('y', sS)
                wait('y', sV, h_rdy)
                dma('y', d_dln2, lnb[:, :], sOut)
                dma('y', d_dh, hbuf[:, :], sOut)
                dma('y', d_dxs, xs[:, :], sOut)
                dma('y', d_dhsg[:, 0:512], hsg[0][:, :], sOut)
                dma('y', d_dhsg[:, 512:1024], hsg[1][:, :], sOut)
                dma('y', d_dgsl, gsl[0][:, :], sOut)
                dma('y', d_dusl, usl[0][:, :], sOut)
                dma('y', d_dpgu[:, 0:1024], mt[:, :], sOut)
                dma('y', d_dpgu[:, 1024:2048], vt[:, :], sOut)
            for oc in range(NCH):
                if oc == 0:
                    wait('y', sCg, ag_mark[2])
                    wait('y', sV, h_rdy)
                for pc in range(2):
                    db = (oc * 2 + pc) % 2
                    wait('y', sT, dn_mark[db])
                    n_ic = 22 if pc == 0 else 21
                    rows = d_gw[2][5504 * oc + 2816 * pc:
                                   5504 * oc + 2816 * pc + 128 * n_ic, :] \
                        .rearrange("(c p) m -> p c m", p=P)
                    dma('y', dsl[db][:, 0:128 * n_ic]
                        .rearrange("p (c m) -> p c m", c=n_ic), rows, sFd)
                ldv = sFd.n
                if oc == 0:
                    wait('t', sV, h_rdy)
                    t_pause()
                wait('t', sFd, ldv)
                pb = oc % 2
                for dpos, ic in enumerate(ic_order):
                    pc = 0 if ic < 22 else 1
                    db = (oc * 2 + pc) % 2
                    col = P * (ic - 22 * pc)
                    mm(pdn[pb][:, :], dsl[db][:, col:col + P],
                       hbuf[:, 512 * ic:512 * ic + 512], dpos == 0, dpos == IC - 1,
                       sem=(sT if dpos == IC - 1 else None))
                dn_mark[0] = dn_mark[1] = sT.n
                wait('v', sT)
                v_delay()
                v_op("tensor_add", xs[:, T * oc + q0:T * oc + q0 + 512],
                     xs[:, T * oc + q0:T * oc + q0 + 512], pdn[pb][:, :], sem=sV)
                v_flush()
                wait('t', sV)
            down_end = sT.n
        fp.close()
        ft.close()
        return sV.n

    x_ready = (sYi, ini_y)
    stages = ["a", "f1", "b", "f2"] if upto == "out" else \
        {"x1": ["a"], "x2": ["a", "f1"], "x3": ["a", "f1", "b"], "dbg": ["a"],
         "dbgf": ["a", "f1"], "dbgb": ["a", "f1", "b"]}[upto]

    for st in stages:
        ctx = ExitStack()
        lnb = ctx.enter_context(nc.sbuf_tensor("lnb_" + st, [P, NCH * T], BF16))
        ln_rdy = ln_phase(lnb, x_ready)
        if st == "a":
            mark = attention(0, lnb, ln_rdy)
        elif st == "b":
            mark = attention(1, lnb, ln_rdy)
        elif st == "f1":
            mark = ffn(lnb, ln_rdy, 0)
        else:
            mark = ffn(lnb, ln_rdy, 1)
        x_ready = (sV, mark)
        ctx.close()
        barrier()

    wait('y', x_ready[0], x_ready[1])
    dma('y', d_out.rearrange("(c p) t -> p c t", p=P),
        xs[:, :].rearrange("p (c t) -> p c t", c=NCH), sOut)
    wait('y', sOut, sOut.n)

    with nc.Block() as block:
        @block.tensor
        def _(E):
            for f in streams['t']:
                f(E)

        @block.vector
        def _(E):
            for f in streams['v']:
                f(E)

        @block.scalar
        def _(E):
            for f in streams['s']:
                f(E)

        @block.gpsimd
        def _(E):
            for f in streams['g']:
                f(E)

        @block.sync
        def _(E):
            for f in streams['y']:
                f(E)

    n_inst = sum(len(v) for v in streams.values())
    print(f"[build] upto={upto} instructions ~{n_inst}")
    return nc


def kernel(**inputs):
    in_maps, toks = host_prep(inputs)
    nc = build_nc("out")
    res = run_bass_kernel_spmd(nc, in_maps, core_ids=list(range(8)))
    out = np.zeros((2, 4096, 2048), np.float32)
    for c, r in enumerate(res.results):
        g, tok = toks[c]
        out[g][tok] = np.asarray(r["out"]).T
    return out
